# version 03
# baseline (speedup 1.0000x reference)
import numpy as np
import ml_dtypes
import concourse.bass as bass
import concourse.mybir as mybir
from concourse.bass_utils import run_bass_kernel_spmd

F32 = mybir.dt.float32
BF16 = mybir.dt.bfloat16
I32 = mybir.dt.int32
AF = mybir.ActivationFunctionType
ALU = mybir.AluOpType
AX = mybir.AxisListType

P = 128
T = 8192
C = 1024
NS = 8
SUP = 1024
NPAIR = 8
I_FFN = 2816
COVER = (4, 8)
DIAGPOS = (3, 7)
NTOK = 2048
LNEPS = 1e-5
NIC = I_FFN // P

BF2 = np.dtype(ml_dtypes.bfloat16)

QT_C = 0
LBQ_C = 16384
LBK_C = 17408
LBV_C = 18432
LAT_C = 19456
UBP1_C = 19712
MSK_C = 16384
WKR_C = 24576
WVR_C = 32768
KT0_C = 24576
KT1_C = 32768
YNT_C = 24576
XTB0_C = 40960
XTB1_C = 49280
V0_C = 40960
V1_C = 49280
EA0_C = 57600
EA1_C = 58624
EB0_C = 59648
EB1_C = 60672
U2B_C = 61696
XQR_C = 62720
KSTG0_C = 63744
KSTG1_C = 64768
VSTG0_C = 65792
VSTG1_C = 66832
ONE_C = 67872
LAPT_C = 67873
LBP_C = 68129
YN_STG_C = 69153
ARB_N = 70400
XQT_C = 0
WBG0_C = 16384
WBG1_C = 20480
WBU0_C = 24576
WBU1_C = 28672
HQS_C = 32768
WDQ3_C = 35584
HQT0_C = 38400
HQT1_C = 40448
WDT0_C = 42496
WDT1_C = 43520

YACC_F = 0
SENT_F = 8320
CNEG_F = 8336
BIAS_F = 8344
KMX_F = 8472
QMX_F = 8536
KM2_F = 8552
QM2_F = 8560
CSC_F = 8568
BN_F = 8576
PSTG_F = 8600
PSTG2_F = 9624
QSA_F = 10648
QSB_F = 11672
QSC_F = 12696
WQS_F = 13720
DUMP_F = 14744
SM_F = 15256
ARF_N = 16384
RECN_F = SM_F + 280
H0_F = 0
H1_F = 2816
Q31_F = 5632
Q32_F = 8448
Q33_F = 11264
X1S_F = 14080
R3A_F = DUMP_F
R3B_F = DUMP_F + 176
BN3_F = SM_F + 400
PSTG3_F = QSA_F
SIL_F = 15800


def _perm_for_core(g):
    d0, d1 = g, 7 - g
    others = [s for s in range(8) if s not in (d0, d1)]
    first = [s for s in others if s < d0]
    rest = [s for s in others if s >= d0]
    p012 = (first + rest)[:3]
    remaining = [s for s in others if s not in p012]
    perm = p012 + [d0] + remaining + [d1]
    assert len(perm) == 8 and sorted(perm) == list(range(8))
    return perm


def _host_prepare(inp):
    x = np.asarray(inp["L0_attn_x"], np.float32)
    w_attn = np.asarray(inp["L0_attn_w_attn"], np.float32)
    b_attn = np.asarray(inp["L0_attn_b_attn"], np.float32)
    la = np.asarray(inp["L0_attn_la_attn"], np.float32)
    lb = np.asarray(inp["L0_attn_lb_attn"], np.float32)
    w_proj = np.asarray(inp["L0_attn_w_proj"], np.float32)
    b_proj = np.asarray(inp["L0_attn_b_proj"], np.float32)
    lap = np.asarray(inp["L0_attn_la_proj"], np.float32)
    lbp = np.asarray(inp["L0_attn_lb_proj"], np.float32)

    shared = {
        "wqT": np.ascontiguousarray(w_attn[0:1024].T),
        "wkT": np.ascontiguousarray(w_attn[1024:2048].T),
        "wvT": np.ascontiguousarray(w_attn[2048:3072].T),
        "laT": np.ascontiguousarray(la.T),
        "lbqA": np.ascontiguousarray(
            np.concatenate([lb[0:1024].T, b_attn[None, 0:1024]], 0)),
        "lbkA": np.ascontiguousarray(
            np.concatenate([lb[1024:2048].T, b_attn[None, 1024:2048]], 0)),
        "lbvA": np.ascontiguousarray(
            np.concatenate([lb[2048:3072].T, b_attn[None, 2048:3072]], 0)),
        "wpT": np.ascontiguousarray(w_proj.T),
        "lapT": np.ascontiguousarray(lap.T),
        "lbpA": np.ascontiguousarray(
            np.concatenate([lbp.T, b_proj[None, :]], 0)),
        "wg": np.asarray(inp["L0_ffn_w_gate"], np.float32),
        "wu": np.asarray(inp["L0_ffn_w_up"], np.float32),
        "wd": np.asarray(inp["L0_ffn_w_down"], np.float32),
    }
    kk = np.arange(P)[:, None]
    qq = np.arange(SUP)[None, :]
    m = np.empty((P, NS * SUP), np.float32)
    for j in range(8):
        m[:, j * SUP:(j + 1) * SUP] = ((j * P + kk) <= qq)
    shared["dmask"] = m.astype(BF2)

    in_maps = []
    for core in range(8):
        b, g = core // 4, core % 4
        perm = _perm_for_core(g)
        xT = x[b].T
        d = dict(shared)
        d["xTperm"] = np.ascontiguousarray(
            xT.reshape(C, NS, SUP)[:, perm, :].reshape(C, T))
        own = [g, 7 - g]
        d["xTown"] = np.ascontiguousarray(
            xT.reshape(C, NS, SUP)[:, own, :].reshape(C, NTOK))
        d["xres"] = np.ascontiguousarray(
            x[b].reshape(NS, SUP, C)[own].reshape(NTOK, C))
        sent = np.full((P, 16), -1.0e30, np.float32)
        for slot in range(2):
            qs = own[slot]
            for pos in range(COVER[slot]):
                s_at = perm[pos]
                valid = (s_at < qs) or (pos == DIAGPOS[slot])
                sent[:, slot * 8 + pos] = 1.0e30 if valid else -1.0e30
        d["sent"] = sent
        in_maps.append(d)
    return in_maps


class _Sem:
    def __init__(self, handle):
        self.h = handle
        self.v = 0


def build_program(debug=False):
    nc = bass.Bass("TRN2", target_bir_lowering=False, debug=False)
    ap = {}

    def din(name, shape, dt):
        ap[name] = nc.dram_tensor(name, shape, dt, kind="ExternalInput").ap()

    def dint(name, shape, dt):
        ap[name] = nc.dram_tensor(name, shape, dt, kind="Internal").ap()

    din("xTperm", [C, T], F32)
    din("xTown", [C, NTOK], F32)
    din("xres", [NTOK, C], F32)
    din("wqT", [C, C], F32)
    din("wkT", [C, C], F32)
    din("wvT", [C, C], F32)
    din("laT", [C, 32], F32)
    din("lbqA", [33, C], F32)
    din("lbkA", [33, C], F32)
    din("lbvA", [33, C], F32)
    din("wpT", [C, C], F32)
    din("lapT", [C, 32], F32)
    din("lbpA", [33, C], F32)
    din("wg", [I_FFN, C], F32)
    din("wu", [I_FFN, C], F32)
    din("wd", [C, I_FFN], F32)
    din("dmask", [P, NS * SUP], BF16)
    din("sent", [P, 16], F32)

    dint("kt_dram", [NS, NPAIR, P, SUP], BF16)
    dint("v_dram", [NS, 8, P, 16 * 65], BF16)
    dint("cneg_dram", [1, NPAIR], F32)
    dint("x1_dram", [NTOK, C], F32)
    dint("xq_dram", [NTOK, C], BF16)
    dint("ynorm_dram", [2, SUP, C], BF16)
    dint("hq_dram", [NTOK, I_FFN], BF16)
    dint("wgq_dram", [I_FFN, C], BF16)
    dint("wuq_dram", [I_FFN, C], BF16)
    dint("wdq_dram", [C, I_FFN], BF16)
    dint("wgqT_dram", [8, P, I_FFN], BF16)
    dint("wuqT_dram", [8, P, I_FFN], BF16)
    dint("wdqT_dram", [NIC, P, C], BF16)
    dint("hqT_dram", [NIC, P, NTOK], BF16)

    ap["out"] = nc.dram_tensor("out", [NTOK, C], F32,
                               kind="ExternalOutput").ap()
    if debug:
        for nm, shp, dt in [("dbg_p", [NTOK, C], F32),
                            ("dbg_x1", [NTOK, C], F32),
                            ("dbg_xq", [NTOK, C], BF16),
                            ("dbg_cneg", [1, NPAIR], F32),
                            ("dbg_yn", [2, SUP, C], BF16)]:
            ap[nm] = nc.dram_tensor(nm, shp, dt, kind="ExternalOutput").ap()

    from contextlib import ExitStack
    with ExitStack() as ctx:
        e = ctx.enter_context
        ARB = e(nc.sbuf_tensor("ARB", [P, ARB_N], BF16))
        ARF = e(nc.sbuf_tensor("ARF", [P, ARF_N], F32))

        def bf(c0, n, p0=0, prows=P):
            return ARB[p0:p0 + prows, c0:c0 + n]

        def ff(c0, n, p0=0, prows=P):
            return ARF[p0:p0 + prows, c0:c0 + n]

        PS_SA = e(nc.psum_tensor("PS_SA", [P, 1024], F32))
        PS_SB = e(nc.psum_tensor("PS_SB", [P, 1024], F32))
        PS_Y0 = e(nc.psum_tensor("PS_Y0", [P, 1024], F32))
        PS_Y1 = e(nc.psum_tensor("PS_Y1", [P, 1024], F32))
        PSY = (PS_Y0, PS_Y1)

        sems = {}
        for name in ("dS", "dG", "dC", "pe", "ac", "dv", "gp"):
            sems[name] = _Sem(e(nc.semaphore(name)))
        block = e(nc.Block())

        ops = {k: [] for k in ("sync", "gpsimd", "tensor", "vector", "scalar")}

        def W(eng, sem, val):
            if val is None or val <= 0:
                return
            h = sems[sem].h
            ops[eng].append(lambda en: en.wait_ge(h, val))

        def dma(eng, sem, outap, inap, transpose=False):
            s = sems[sem]
            s.v += 16
            h = s.h
            if transpose:
                ops[eng].append(
                    lambda en: en.dma_start_transpose(outap, inap)
                    .then_inc(h, 16))
            else:
                ops[eng].append(
                    lambda en: en.dma_start(out=outap, in_=inap)
                    .then_inc(h, 16))
            return s.v

        def _mk(eng, sem):
            def run(fn, inc=True):
                if inc:
                    s = sems[sem]
                    s.v += 1
                    h = s.h
                    ops[eng].append(lambda en: fn(en).then_inc(h, 1))
                    return s.v
                ops[eng].append(fn)
                return None
            return run

        pe = _mk("tensor", "pe")
        acc = _mk("scalar", "ac")
        dvv = _mk("vector", "dv")
        gpc = _mk("gpsimd", "gp")

        def drain(eng):
            ops[eng].append(lambda en: en.drain())

        v_sent = dma("sync", "dS", ff(SENT_F, 16), ap["sent"][:, :])
        dma("gpsimd", "dG", bf(LAPT_C, 256).rearrange("p (a n) -> p a n", a=8),
            ap["lapT"].rearrange("(a p) r -> p a r", p=P))
        dma("gpsimd", "dG", bf(LBP_C, 1024, 0, 33), ap["lbpA"][:, :])
        dma("gpsimd", "dG", bf(LBQ_C, 1024, 0, 33), ap["lbqA"][:, :])
        dma("gpsimd", "dG", bf(LBK_C, 1024, 0, 33), ap["lbkA"][:, :])
        dma("gpsimd", "dG", bf(LBV_C, 1024, 0, 33), ap["lbvA"][:, :])
        dma("gpsimd", "dG", bf(LAT_C, 256).rearrange("p (a n) -> p a n", a=8),
            ap["laT"].rearrange("(a p) r -> p a r", p=P))
        gpc(lambda en: en.memset(bf(ONE_C, 1), 1.0), inc=False)
        gpc(lambda en: en.memset(bf(UBP1_C, 1024, 32, 1), 1.0), inc=False)
        gpc(lambda en: en.memset(bf(U2B_C, 1024, 32, 1), 1.0), inc=False)
        gpc(lambda en: en.memset(bf(VSTG0_C, 1040).rearrange(
            "p (h w) -> p h w", w=65)[:, :, 64:65], 1.0), inc=False)
        v_ms = gpc(lambda en: en.memset(bf(VSTG1_C, 1040).rearrange(
            "p (h w) -> p h w", w=65)[:, :, 64:65], 1.0))
        v_wk = dma("gpsimd", "dG", bf(WKR_C, 8192).rearrange("p (a n) -> p a n", a=8),
                   ap["wkT"].rearrange("(a p) o -> p a o", p=P))
        v_wv = dma("gpsimd", "dG", bf(WVR_C, 8192).rearrange("p (a n) -> p a n", a=8),
                   ap["wvT"].rearrange("(a p) o -> p a o", p=P))

        xtb = [bf(XTB0_C, 8192), bf(XTB1_C, 8192)]
        kstg = [bf(KSTG0_C, 1024), bf(KSTG1_C, 1024)]
        vstg = [bf(VSTG0_C, 1040), bf(VSTG1_C, 1040)]
        sqb = bf(XQR_C, 1024)
        ub = bf(UBP1_C, 1024, 0, 33)
        u2b = bf(U2B_C, 1024, 0, 33)
        wkr = bf(WKR_C, 8192).rearrange("p (a o) -> p a o", a=8)
        wvr = bf(WVR_C, 8192).rearrange("p (a o) -> p a o", a=8)
        laTb = bf(LAT_C, 256).rearrange("p (a r) -> p a r", a=8)
        lbq = bf(LBQ_C, 1024, 0, 33)
        lbk = bf(LBK_C, 1024, 0, 33)
        lbv = bf(LBV_C, 1024, 0, 33)
        lbp = bf(LBP_C, 1024, 0, 33)
        onec = bf(ONE_C, 1)

        def xsrc(pos):
            return ap["xTperm"][:, pos * SUP:(pos + 1) * SUP].rearrange(
                "(a p) t -> p a t", p=P)

        st = {
            "ub_free_pe": 0,
            "ulora_ps_free_ac": 0,
            "sa_free_ac": [0, 0],
            "kstg_free_dc": [0, 0],
            "vstg_free_dc": [0, 0],
            "norm_ps_free_dv": 0,
            "sqb_free_pe": 0,
            "xtb_last_pe": [0, 0],
            "w_last_pe": 0,
        }

        def emit_u_lora(xtbuf):
            xt3 = xtbuf.rearrange("p (a t) -> p a t", a=8)
            W("tensor", "ac", st["ulora_ps_free_ac"])
            vpe = None
            for half in range(2):
                for a in range(8):
                    def mm(en, a=a, half=half, xt3=xt3):
                        return en.matmul(
                            PS_Y0[0:32, half * 512:(half + 1) * 512],
                            laTb[:, a, :],
                            xt3[:, a, half * 512:(half + 1) * 512],
                            start=(a == 0), stop=(a == 7))
                    if half == 1 and a == 7:
                        vpe = pe(mm)
                    else:
                        pe(mm, inc=False)
            W("scalar", "pe", vpe)
            W("scalar", "pe", st["ub_free_pe"])
            acc(lambda en: en.activation(ub[0:32, 0:512],
                                         PS_Y0[0:32, 0:512], AF.Copy,
                                         scale=0.5), inc=False)
            vac = acc(lambda en: en.activation(ub[0:32, 512:1024],
                                               PS_Y0[0:32, 512:1024], AF.Copy,
                                               scale=0.5))
            st["ulora_ps_free_ac"] = vac
            return vac, vpe

        def emit_norms(vac_sq, col):
            W("tensor", "ac", vac_sq)
            W("tensor", "dv", st["norm_ps_free_dv"])
            pe(lambda en: en.matmul(PS_Y1[0:1, 0:512], onec,
                                    sqb[:, 0:512], start=True, stop=True),
               inc=False)
            vpeN = pe(lambda en: en.matmul(PS_Y1[0:1, 512:1024], onec,
                                           sqb[:, 512:1024],
                                           start=True, stop=True))
            st["sqb_free_pe"] = vpeN
            W("vector", "pe", vpeN)
            dst = ff(col, 1, 0, 1)
            dvv(lambda en: en.tensor_scalar(
                ff(DUMP_F, 512, 0, 1), PS_Y1[0:1, 0:512], 1.0, None,
                ALU.mult, ALU.max, accum_out=dst), inc=False)
            drain("vector")
            vdvN = dvv(lambda en: en.tensor_scalar(
                ff(DUMP_F, 512, 0, 1), PS_Y1[0:1, 512:1024], 1.0, dst,
                ALU.mult, ALU.max, accum_out=dst))
            drain("vector")
            st["norm_ps_free_dv"] = vdvN

        ld_x = [dma("gpsimd", "dG", xtb[0].rearrange("p (a n) -> p a n", a=8), xsrc(0)), 0]
        W("tensor", "dG", v_wv)

        for pos in range(NS):
            buf = pos % 2
            if pos + 1 < NS:
                W("gpsimd", "pe", st["xtb_last_pe"][1 - buf])
                ld_x[1 - buf] = dma(
                    "gpsimd", "dG",
                    xtb[1 - buf].rearrange("p (a n) -> p a n", a=8),
                    xsrc(pos + 1))
            W("tensor", "dG", ld_x[buf])
            xt3 = xtb[buf].rearrange("p (a t) -> p a t", a=8)
            vac_u, _ = emit_u_lora(xtb[buf])
            W("tensor", "ac", vac_u)

            for pair in range(NPAIR):
                sa = pair % 2
                ps = PS_SA if sa == 0 else PS_SB
                W("tensor", "ac", st["sa_free_ac"][sa])
                vpeK = None
                for half in range(2):
                    for a in range(8):
                        def mm(en, a=a, half=half, ps=ps, pair=pair, xt3=xt3):
                            return en.matmul(
                                ps[:, half * 512:(half + 1) * 512],
                                wkr[:, a, pair * 128:(pair + 1) * 128],
                                xt3[:, a, half * 512:(half + 1) * 512],
                                start=(a == 0), stop=False)
                        pe(mm, inc=False)
                    def mml(en, half=half, ps=ps, pair=pair):
                        return en.matmul(
                            ps[:, half * 512:(half + 1) * 512],
                            lbk[:, pair * 128:(pair + 1) * 128],
                            ub[:, half * 512:(half + 1) * 512],
                            start=False, stop=True)
                    if half == 1:
                        vpeK = pe(mml)
                    else:
                        pe(mml, inc=False)
                st["ub_free_pe"] = vpeK
                st["xtb_last_pe"][buf] = vpeK
                st["w_last_pe"] = vpeK
                kb = pair % 2
                W("scalar", "pe", vpeK)
                W("scalar", "dC", st["kstg_free_dc"][kb])
                acc(lambda en, ps=ps, kb=kb: en.activation(
                    kstg[kb][:, 0:512], ps[:, 0:512], AF.Copy), inc=False)
                acc(lambda en, ps=ps, kb=kb: en.activation(
                    kstg[kb][:, 512:1024], ps[:, 512:1024], AF.Copy),
                    inc=False)
                W("scalar", "pe", st["sqb_free_pe"])
                acc(lambda en, ps=ps: en.activation(
                    sqb[:, 0:512], ps[:, 0:512], AF.Square), inc=False)
                vacS = acc(lambda en, ps=ps: en.activation(
                    sqb[:, 512:1024], ps[:, 512:1024], AF.Square))
                st["sa_free_ac"][sa] = vacS
                st["kstg_free_dc"][kb] = dma(
                    "scalar", "dC", ap["kt_dram"][pos, pair, :, :],
                    kstg[kb][:, :])
                emit_norms(vacS, KMX_F + pair * 8 + pos)

            for tsub in range(8):
                sa = tsub % 2
                ps = PS_SA if sa == 0 else PS_SB
                W("tensor", "ac", st["sa_free_ac"][sa])
                vpeV = None
                for half in range(2):
                    for a in range(8):
                        def mm(en, a=a, half=half, ps=ps, tsub=tsub, xt3=xt3):
                            return en.matmul(
                                ps[:, half * 512:(half + 1) * 512],
                                xt3[:, a, tsub * 128:(tsub + 1) * 128],
                                wvr[:, a, half * 512:(half + 1) * 512],
                                start=(a == 0), stop=False)
                        pe(mm, inc=False)
                    def mml(en, half=half, ps=ps, tsub=tsub):
                        return en.matmul(
                            ps[:, half * 512:(half + 1) * 512],
                            ub[:, tsub * 128:(tsub + 1) * 128],
                            lbv[:, half * 512:(half + 1) * 512],
                            start=False, stop=True)
                    if half == 1:
                        vpeV = pe(mml)
                    else:
                        pe(mml, inc=False)
                st["ub_free_pe"] = vpeV
                st["xtb_last_pe"][buf] = vpeV
                vb = tsub % 2
                W("scalar", "pe", vpeV)
                W("scalar", "dC", st["vstg_free_dc"][vb])
                if pos == 0 and tsub < 2:
                    W("scalar", "gp", v_ms)
                vs3 = vstg[vb].rearrange("p (h w) -> p h w", w=65)
                acc(lambda en, ps=ps, vs3=vs3: en.activation(
                    vs3[:, 0:8, 0:64],
                    ps[:, 0:512].rearrange("p (h d) -> p h d", d=64),
                    AF.Copy), inc=False)
                vacV = acc(lambda en, ps=ps, vs3=vs3: en.activation(
                    vs3[:, 8:16, 0:64],
                    ps[:, 512:1024].rearrange("p (h d) -> p h d", d=64),
                    AF.Copy))
                st["sa_free_ac"][sa] = vacV
                st["vstg_free_dc"][vb] = dma(
                    "scalar", "dC", ap["v_dram"][pos, tsub, :, :],
                    vstg[vb][:, :])

        W("gpsimd", "pe", st["w_last_pe"])
        v_wq = dma("gpsimd", "dG", bf(WKR_C, 8192).rearrange("p (a n) -> p a n", a=8),
                   ap["wqT"].rearrange("(a p) o -> p a o", p=P))
        wqr = wkr
        for os_ in range(2):
            W("gpsimd", "pe", st["xtb_last_pe"][os_])
            ld = dma("gpsimd", "dG",
                     xtb[os_].rearrange("p (a n) -> p a n", a=8),
                     ap["xTown"][:, os_ * SUP:(os_ + 1) * SUP]
                     .rearrange("(a p) t -> p a t", p=P))
            if os_ == 0:
                W("tensor", "dG", v_wq)
            W("tensor", "dG", ld)
            xt3 = xtb[os_].rearrange("p (a t) -> p a t", a=8)
            vac_u, _ = emit_u_lora(xtb[os_])
            W("tensor", "ac", vac_u)
            for pair in range(NPAIR):
                sa = pair % 2
                ps = PS_SA if sa == 0 else PS_SB
                W("tensor", "ac", st["sa_free_ac"][sa])
                vpeQ = None
                for half in range(2):
                    for a in range(8):
                        def mm(en, a=a, half=half, ps=ps, pair=pair, xt3=xt3):
                            return en.matmul(
                                ps[:, half * 512:(half + 1) * 512],
                                wqr[:, a, pair * 128:(pair + 1) * 128],
                                xt3[:, a, half * 512:(half + 1) * 512],
                                start=(a == 0), stop=False)
                        pe(mm, inc=False)
                    def mml(en, half=half, ps=ps, pair=pair):
                        return en.matmul(
                            ps[:, half * 512:(half + 1) * 512],
                            lbq[:, pair * 128:(pair + 1) * 128],
                            ub[:, half * 512:(half + 1) * 512],
                            start=False, stop=True)
                    if half == 1:
                        vpeQ = pe(mml)
                    else:
                        pe(mml, inc=False)
                st["ub_free_pe"] = vpeQ
                st["xtb_last_pe"][os_] = vpeQ
                W("scalar", "pe", vpeQ)
                qtdst = bf(QT_C + pair * 2048 + os_ * 1024, 1024)
                acc(lambda en, ps=ps, qtdst=qtdst: en.activation(
                    qtdst[:, 0:512], ps[:, 0:512], AF.Copy), inc=False)
                acc(lambda en, ps=ps, qtdst=qtdst: en.activation(
                    qtdst[:, 512:1024], ps[:, 512:1024], AF.Copy), inc=False)
                W("scalar", "pe", st["sqb_free_pe"])
                acc(lambda en, ps=ps: en.activation(
                    sqb[:, 0:512], ps[:, 0:512], AF.Square), inc=False)
                vacS = acc(lambda en, ps=ps: en.activation(
                    sqb[:, 512:1024], ps[:, 512:1024], AF.Square))
                st["sa_free_ac"][sa] = vacS
                emit_norms(vacS, QMX_F + pair * 2 + os_)

        drain("vector")
        for pair in range(NPAIR):
            dvv(lambda en, pair=pair: en.tensor_reduce(
                ff(KM2_F + pair, 1, 0, 1), ff(KMX_F + pair * 8, 8, 0, 1),
                AX.X, ALU.max), inc=False)
            dvv(lambda en, pair=pair: en.tensor_reduce(
                ff(QM2_F + pair, 1, 0, 1), ff(QMX_F + pair * 2, 2, 0, 1),
                AX.X, ALU.max), inc=False)
        drain("vector")
        vdv = dvv(lambda en: en.tensor_tensor(
            ff(CSC_F, 8, 0, 1), ff(KM2_F, 8, 0, 1), ff(QM2_F, 8, 0, 1),
            ALU.mult))
        W("scalar", "dv", vdv)
        acc(lambda en: en.activation(ff(CSC_F, 8, 0, 1),
                                     ff(CSC_F, 8, 0, 1), AF.Ln), inc=False)
        drain("scalar")
        vac = acc(lambda en: en.activation(
            ff(CSC_F, 8, 0, 1), ff(CSC_F, 8, 0, 1), AF.Exp, scale=0.5))
        W("vector", "ac", vac)
        drain("vector")
        vdv = dvv(lambda en: en.tensor_scalar(
            ff(CSC_F, 8, 0, 1), ff(CSC_F, 8, 0, 1), -0.126256, None,
            ALU.mult))
        drain("vector")
        W("scalar", "dv", vdv)
        vst = dma("scalar", "dC", ap["cneg_dram"][:, :], ff(CSC_F, 8, 0, 1))
        if debug:
            dma("scalar", "dC", ap["dbg_cneg"][:, :], ff(CSC_F, 8, 0, 1))
        W("sync", "dC", vst)
        vld = dma("sync", "dS", ff(CNEG_F, 8),
                  ap["cneg_dram"][0:1, :].broadcast_to([P, 8]))
        W("vector", "dS", vld)
        W("vector", "dS", v_sent)
        for slot in range(2):
            for pos in range(COVER[slot]):
                for pair in range(NPAIR):
                    idx = BIAS_F + slot * 64 + pos * 8 + pair
                    dvv(lambda en, idx=idx, pair=pair, slot=slot, pos=pos:
                        en.tensor_scalar(
                            ff(idx, 1), ff(CNEG_F + pair, 1),
                            ff(SENT_F + slot * 8 + pos, 1), None, ALU.min),
                        inc=False)
        drain("vector")
        vdv_bias = dvv(lambda en: en.tensor_scalar(
            ff(SM_F + 500, 1), ff(BIAS_F, 1), 1.0, None, ALU.mult))

        W("sync", "pe", st["ub_free_pe"])
        v_msk = dma("sync", "dS", bf(MSK_C, 8192), ap["dmask"][:, :])
        W("sync", "dC", sems["dC"].v)

        ktb = [bf(KT0_C, 8192), bf(KT1_C, 8192)]
        vbuf = [bf(V0_C, 8320), bf(V1_C, 8320)]
        Ebuf = [[bf(EA0_C, 1024), bf(EA1_C, 1024)],
                [bf(EB0_C, 1024), bf(EB1_C, 1024)]]

        wq_tiles = []
        for wname, dst in (("wg", "wgq_dram"), ("wu", "wuq_dram")):
            for r in range(NIC):
                wq_tiles.append((ap[wname][r * P:(r + 1) * P, :],
                                 ap[dst][r * P:(r + 1) * P, :]))
        wq_state = {"i": 0, "stg_free": 0, "obf_free": 0}

        def emit_quant(src, dst, width, s1, s2, s3, mblk, sblk, rblk, s2blk):
            nb = width // 16
            x3 = src.rearrange("p (b k) -> p b k", k=16)
            dvv(lambda en: en.tensor_reduce(mblk, x3, AX.X, ALU.max,
                                            apply_absolute_value=True),
                inc=False)
            drain("vector")
            dvv(lambda en: en.tensor_scalar(sblk, mblk, 1.0 / 6.0, 1e-12,
                                            ALU.mult, ALU.max), inc=False)
            drain("vector")
            dvv(lambda en: en.reciprocal(rblk, sblk), inc=False)
            dvv(lambda en: en.tensor_scalar(s2blk, sblk, 2.0, None, ALU.mult),
                inc=False)
            drain("vector")
            dvv(lambda en: en.tensor_tensor(
                s1.rearrange("p (b k) -> p b k", k=16), x3,
                rblk.unsqueeze(-1).broadcast_to([P, nb, 16]), ALU.mult),
                inc=False)
            drain("vector")
            dvv(lambda en: en.tensor_scalar(
                s2.bitcast(I32), s1.bitcast(I32), 0x7FFFFFFF, None,
                ALU.bitwise_and), inc=False)
            drain("vector")
            dvv(lambda en: en.tensor_scalar(s3, s2, 1.0, None, ALU.max),
                inc=False)
            drain("vector")
            dvv(lambda en: en.tensor_scalar(
                s3.bitcast(I32), s3.bitcast(I32), 8, None,
                ALU.logical_shift_right), inc=False)
            drain("vector")
            dvv(lambda en: en.tensor_scalar(
                s3.bitcast(I32), s3.bitcast(I32), 8192.0, None, ALU.add),
                inc=False)
            drain("vector")
            dvv(lambda en: en.tensor_scalar(
                s3.bitcast(I32), s3.bitcast(I32), 0xFFC000, 8,
                ALU.bitwise_and, ALU.logical_shift_left), inc=False)
            drain("vector")
            dvv(lambda en: en.tensor_scalar(
                s1, s2, 0.25, -0.5, ALU.is_le, ALU.mult), inc=False)
            drain("vector")
            dvv(lambda en: en.tensor_scalar(
                s2, s2, 0.75, -0.5, ALU.is_le, ALU.mult), inc=False)
            drain("vector")
            dvv(lambda en: en.tensor_tensor(s1, s1, s2, ALU.add), inc=False)
            drain("vector")
            dvv(lambda en: en.tensor_tensor(s3, s3, s1, ALU.add), inc=False)
            dvv(lambda en: en.tensor_scalar(
                s2, src, 0.0, -0.5, ALU.is_ge, ALU.add), inc=False)
            drain("vector")
            dvv(lambda en: en.tensor_tensor(
                s2.rearrange("p (b k) -> p b k", k=16),
                s2.rearrange("p (b k) -> p b k", k=16),
                s2blk.unsqueeze(-1).broadcast_to([P, nb, 16]), ALU.mult),
                inc=False)
            drain("vector")
            v = dvv(lambda en: en.tensor_tensor(dst, s3, s2, ALU.mult))
            drain("vector")
            return v

        def emit_wq_step(n):
            for _ in range(n):
                i = wq_state["i"]
                if i >= len(wq_tiles):
                    return
                wq_state["i"] = i + 1
                src, dst = wq_tiles[i]
                stg = ff(WQS_F, 1024)
                obf = bf(XQR_C, 1024)
                W("sync", "dv", wq_state["stg_free"])
                ld = dma("sync", "dS", stg, src)
                W("vector", "dS", ld)
                W("vector", "dC", wq_state["obf_free"])
                vq = emit_quant(stg, obf, 1024,
                                ff(QSA_F, 1024), ff(QSB_F, 1024),
                                ff(QSC_F, 1024),
                                ff(SM_F, 64), ff(SM_F + 64, 64),
                                ff(SM_F + 128, 64), ff(SM_F + 192, 64))
                wq_state["stg_free"] = vq
                W("scalar", "dv", vq)
                wq_state["obf_free"] = dma("scalar", "dC", dst, obf)

        def qtv(pair, slot, r0, r1):
            return bf(QT_C + pair * 2048 + slot * 1024, 1024, r0, r1 - r0)

        def prime(n=24):
            for i in range(n):
                def mm(en, i=i):
                    return en.matmul(PS_SA[:, 0:512], bf(QT_C, 128),
                                     bf(QT_C, 512), start=True, stop=True)
                pe(mm, inc=False)
            return pe(lambda en: en.matmul(PS_SA[:, 0:512], bf(QT_C, 128),
                                           bf(QT_C, 512), start=True,
                                           stop=True))

        st2 = {
            "eb": 0,
            "e_free_pe": [[0, 0], [0, 0]],
            "s_free": [st["sa_free_ac"][0], st["sa_free_ac"][1]],
            "psY_evac_dv": st["norm_ps_free_dv"],
            "psY0_ac": st["ulora_ps_free_ac"],
            "x1q_free": 0,
            "kt_last_pe": [0, 0],
            "proj_last_pe": 0,
        }
        kv_ld = {}

        def load_kv(pos, buf):
            W("sync", "pe", st2["kt_last_pe"][buf])
            dma("sync", "dS", ktb[buf].rearrange("p (r n) -> p r n", r=8),
                ap["kt_dram"][pos].rearrange("r p n -> p r n"))
            return dma("sync", "dS",
                       vbuf[buf].rearrange("p (s n) -> p s n", s=8),
                       ap["v_dram"][pos].rearrange("s p n -> p s n"))

        kv_ld[(0, 0)] = load_kv(0, 0)
        W("tensor", "dS", v_msk)
        W("scalar", "dv", vdv_bias)
        W("tensor", "ac", st2["psY0_ac"])

        def attention_slot(slot):
            cov = COVER[slot]
            W("tensor", "ac", st2["psY0_ac"])
            W("tensor", "ac", st2["s_free"][0])
            vpr = prime()
            W("scalar", "pe", vpr)
            for pos in range(cov):
                buf = pos % 2
                if (slot, pos) not in kv_ld:
                    kv_ld[(slot, pos)] = load_kv(pos, buf)
                if pos + 1 < cov:
                    kv_ld[(slot, pos + 1)] = load_kv(pos + 1, 1 - buf)
                W("tensor", "dS", kv_ld[(slot, pos)])
                kt3 = ktb[buf].rearrange("p (r n) -> p r n", r=8)
                v3 = vbuf[buf].rearrange("p (s n) -> p s n", s=8)
                diag = pos == DIAGPOS[slot]
                vpeAV = None

                def emit_av(ksub, eb, gate, pair, v3, first):
                    if gate[0] is not None:
                        W("tensor", gate[0], gate[1])
                    if first:
                        W("tensor", "dv", st2["psY_evac_dv"])
                    vv = None
                    for qsub in range(8):
                        pst = PSY[qsub // 4]
                        coff = (0, 130, 512, 642)[qsub % 4]
                        fb = (first and qsub % 2 == 0)
                        for h in range(2):
                            def av(en, h=h, qsub=qsub, ksub=ksub, eb=eb,
                                   pst=pst, coff=coff, v3=v3, pair=pair,
                                   fb=fb):
                                return en.matmul(
                                    pst[:, coff + h * 65:coff + h * 65 + 65],
                                    Ebuf[h][eb]
                                    [:, qsub * 128:(qsub + 1) * 128],
                                    v3[:, ksub,
                                       (pair * 2 + h) * 65:
                                       (pair * 2 + h) * 65 + 65],
                                    start=(fb and h == 0), stop=(ksub == 7))
                            if h == 1 and qsub == 7:
                                vv = pe(av)
                            else:
                                pe(av, inc=False)
                    st2["e_free_pe"][0][eb] = vv
                    st2["e_free_pe"][1][eb] = vv
                    return vv

                for pair in range(NPAIR):
                    bias = ff(BIAS_F + slot * 64 + pos * 8 + pair, 1)
                    pend = None
                    for ksub in range(8):
                        eb = st2["eb"]
                        st2["eb"] = 1 - eb
                        vpeSA = vpeSB = None
                        W("tensor", "ac", st2["s_free"][0])
                        for half in range(2):
                            def mmSA(en, half=half, pair=pair, ksub=ksub,
                                     kt3=kt3, slot=slot):
                                return en.matmul(
                                    PS_SA[:, half * 512:(half + 1) * 512],
                                    kt3[0:64, pair,
                                        ksub * 128:(ksub + 1) * 128],
                                    qtv(pair, slot, 0, 64)
                                    [:, half * 512:(half + 1) * 512],
                                    start=True, stop=True,
                                    tile_position=(0, 0))
                            if half == 1:
                                vpeSA = pe(mmSA)
                            else:
                                pe(mmSA, inc=False)
                        W("tensor", "ac", st2["s_free"][1])
                        for half in range(2):
                            def mmSB(en, half=half, pair=pair, ksub=ksub,
                                     kt3=kt3, slot=slot):
                                return en.matmul(
                                    PS_SB[:, half * 512:(half + 1) * 512],
                                    kt3[64:128, pair,
                                        ksub * 128:(ksub + 1) * 128],
                                    qtv(pair, slot, 64, 128)
                                    [:, half * 512:(half + 1) * 512],
                                    start=True, stop=True,
                                    tile_position=(64, 0))
                            if half == 1:
                                vpeSB = pe(mmSB)
                            else:
                                pe(mmSB, inc=False)
                        st2["kt_last_pe"][buf] = vpeSB
                        if pend is not None:
                            vpeAV = emit_av(pend[0], pend[1], pend[2], pair,
                                            v3, pend[0] == 0)
                        W("scalar", "pe", vpeSA)
                        W("scalar", "pe", st2["e_free_pe"][0][eb])
                        vacA = acc(lambda en, eb=eb, bias=bias: en.activation(
                            Ebuf[0][eb][:, :], PS_SA[:, :], AF.Exp,
                            bias=bias, scale=0.125))
                        st2["s_free"][0] = vacA
                        W("scalar", "pe", vpeSB)
                        W("scalar", "pe", st2["e_free_pe"][1][eb])
                        vacB = acc(lambda en, eb=eb, bias=bias: en.activation(
                            Ebuf[1][eb][:, :], PS_SB[:, :], AF.Exp,
                            bias=bias, scale=0.125))
                        st2["s_free"][1] = vacB
                        gate = ("ac", vacB)
                        if diag:
                            W("vector", "ac", vacB)
                            mview = bf(MSK_C + ksub * 1024, 1024)
                            dvv(lambda en, eb=eb, mview=mview:
                                en.tensor_tensor(
                                    Ebuf[0][eb][:, :], Ebuf[0][eb][:, :],
                                    mview, ALU.mult), inc=False)
                            vmask = dvv(lambda en, eb=eb, mview=mview:
                                        en.tensor_tensor(
                                            Ebuf[1][eb][:, :],
                                            Ebuf[1][eb][:, :],
                                            mview, ALU.mult))
                            drain("vector")
                            gate = ("dv", vmask)
                        pend = (ksub, eb, gate)
                    vpeAV = emit_av(pend[0], pend[1], pend[2], pair, v3,
                                    False)
                    W("vector", "pe", vpeAV)
                    vdvY = None
                    for t in range(2):
                        src = PSY[t][:, :].rearrange(
                            "p (a r) -> p a r", a=2)[:, :, 0:260].rearrange(
                            "p a (b c) -> p a b c", c=130)
                        base = YACC_F + ((t * 4) * 8 + pair) * 130
                        dstsl = ARF[:, base:base + 4 * 1040].rearrange(
                            "p (a b r) -> p a b r", a=2, r=1040)[:, :, :, 0:130]
                        if pos == 0:
                            def ev(en, dstsl=dstsl, src=src):
                                return en.tensor_copy(dstsl, src)
                        else:
                            def ev(en, dstsl=dstsl, src=src):
                                return en.tensor_tensor(dstsl, dstsl, src,
                                                        ALU.add)
                        vdvY = dvv(ev)
                    drain("vector")
                    st2["psY_evac_dv"] = vdvY
                emit_wq_step(4)

        wp_ld = {}

        def slot_post(slot):
            yn_st = 0
            for qsub in range(8):
                base = YACC_F + qsub * 8 * 130
                yv = ARF[:, base:base + 8 * 130]
                dens = yv.rearrange("p (r c) -> p r c", c=130)[:, :, 64:130:65]
                rec = ff(RECN_F, 16).rearrange("p (r c) -> p r c", c=2)
                W("vector", "dv", st2["psY_evac_dv"])
                if qsub == 0:
                    drain("vector")
                dvv(lambda en, dens=dens, rec=rec: en.reciprocal(rec, dens),
                    inc=False)
                drain("vector")
                ys = yv.rearrange("p (r hc) -> p r hc", hc=130).rearrange(
                    "p r (h c) -> p r h c", c=65)[:, :, :, 0:64]
                yn = bf(YN_STG_C, 1024).rearrange(
                    "p (r h c) -> p r h c", r=8, c=64)
                if qsub > 0:
                    W("vector", "dC", yn_st)
                vdvN = dvv(lambda en, ys=ys, yn=yn, rec=rec: en.tensor_tensor(
                    yn, ys,
                    rec.unsqueeze(-1).broadcast_to([P, 8, 2, 64]), ALU.mult))
                drain("vector")
                W("scalar", "dv", vdvN)
                yn_st = dma("scalar", "dC",
                            ap["ynorm_dram"][slot, qsub * P:(qsub + 1) * P, :],
                            bf(YN_STG_C, 1024))
                if debug:
                    dma("scalar", "dC",
                        ap["dbg_yn"][slot, qsub * P:(qsub + 1) * P, :],
                        bf(YN_STG_C, 1024))
            W("sync", "dC", yn_st)
            W("sync", "pe", st2["kt_last_pe"][0])
            ynld = 0
            for cc in range(8):
                ynld = dma("sync", "dS", bf(YNT_C + cc * 1024, 1024),
                           ap["ynorm_dram"][slot][:, cc * P:(cc + 1) * P],
                           transpose=True)
            ynt3 = bf(YNT_C, 8192).rearrange("p (a t) -> p a t", a=8)
            W("tensor", "dS", ynld)
            W("tensor", "dv", st2["psY_evac_dv"])
            vpeU = None
            for half in range(2):
                for a in range(8):
                    def mmu(en, a=a, half=half, ynt3=ynt3):
                        return en.matmul(
                            PS_Y0[0:32, half * 512:(half + 1) * 512],
                            bf(LAPT_C + a * 32, 32),
                            ynt3[:, a, half * 512:(half + 1) * 512],
                            start=(a == 0), stop=(a == 7))
                    if half == 1 and a == 7:
                        vpeU = pe(mmu)
                    else:
                        pe(mmu, inc=False)
            W("scalar", "pe", vpeU)
            W("scalar", "pe", st2["proj_last_pe"])
            acc(lambda en: en.activation(u2b[0:32, 0:512],
                                         PS_Y0[0:32, 0:512], AF.Copy,
                                         scale=0.5), inc=False)
            vacU2 = acc(lambda en: en.activation(u2b[0:32, 512:1024],
                                                 PS_Y0[0:32, 512:1024],
                                                 AF.Copy, scale=0.5))
            st2["psY0_ac"] = vacU2
            W("gpsimd", "pe", st2["kt_last_pe"][1])
            for a in range(8):
                wp_ld[(slot, a)] = dma(
                    "gpsimd", "dG", bf(KT1_C + a * 1024, 1024),
                    ap["wpT"][a * P:(a + 1) * P, :])
            W("tensor", "ac", vacU2)
            W("tensor", "ac", st2["s_free"][0])
            prime()
            for tsub in range(8):
                sa = tsub % 2
                ps = PS_SA if sa == 0 else PS_SB
                W("tensor", "ac", st2["s_free"][sa])
                vpeP = None
                for a in range(8):
                    if tsub == 0:
                        W("tensor", "dG", wp_ld[(slot, a)])
                    for half in range(2):
                        def mmp(en, a=a, half=half, ps=ps, tsub=tsub):
                            return en.matmul(
                                ps[:, half * 512:(half + 1) * 512],
                                ynt3[:, a, tsub * 128:(tsub + 1) * 128],
                                bf(KT1_C + a * 1024, 1024)
                                [:, half * 512:(half + 1) * 512],
                                start=(a == 0), stop=False)
                        pe(mmp, inc=False)
                for half in range(2):
                    def mmpl(en, half=half, ps=ps, tsub=tsub):
                        return en.matmul(
                            ps[:, half * 512:(half + 1) * 512],
                            u2b[:, tsub * 128:(tsub + 1) * 128],
                            lbp[:, half * 512:(half + 1) * 512],
                            start=False, stop=True)
                    if half == 1:
                        vpeP = pe(mmpl)
                    else:
                        pe(mmpl, inc=False)
                st2["proj_last_pe"] = vpeP
                st2["kt_last_pe"][0] = vpeP
                st2["kt_last_pe"][1] = vpeP
                W("scalar", "pe", vpeP)
                W("scalar", "dv", st2.get("pstg_free_dv", 0))
                W("scalar", "dC", st2.get("pstg_free_dc", 0))
                acc(lambda en, ps=ps: en.activation(
                    ff(PSTG_F, 512), ps[:, 0:512], AF.Copy), inc=False)
                vacP = acc(lambda en, ps=ps: en.activation(
                    ff(PSTG_F + 512, 512), ps[:, 512:1024], AF.Copy))
                st2["s_free"][sa] = vacP
                row0 = slot * 1024 + tsub * P
                W("sync", "dv", st2.get("pstg2_free_dv", 0))
                vxr = dma("sync", "dS", ff(PSTG2_F, 1024),
                          ap["xres"][row0:row0 + P, :])
                W("vector", "ac", vacP)
                if debug:
                    vdbg = dma("scalar", "dC", ap["dbg_p"][row0:row0 + P, :],
                               ff(PSTG_F, 1024))
                    W("vector", "dC", vdbg)
                W("vector", "dS", vxr)
                vres = dvv(lambda en: en.tensor_tensor(
                    ff(PSTG_F, 1024), ff(PSTG_F, 1024), ff(PSTG2_F, 1024),
                    ALU.add))
                st2["pstg2_free_dv"] = vres
                drain("vector")
                dvv(lambda en: en.bn_stats(ff(BN_F, 6), ff(PSTG_F, 512)),
                    inc=False)
                dvv(lambda en: en.bn_stats(ff(BN_F + 6, 6),
                                           ff(PSTG_F + 512, 512)), inc=False)
                drain("vector")
                vdvS = dvv(lambda en: en.bn_aggr(ff(BN_F + 12, 2),
                                                 ff(BN_F, 12)))
                drain("vector")
                dvv(lambda en: en.tensor_scalar(
                    ff(BN_F + 13, 1), ff(BN_F + 13, 1), LNEPS, None,
                    ALU.add), inc=False)
                drain("vector")
                vdvS = dvv(lambda en: en.tensor_scalar(
                    ff(BN_F + 15, 1), ff(BN_F + 13, 1), 1.0, None, ALU.mult))
                W("scalar", "dv", vdvS)
                acc(lambda en: en.activation(
                    ff(BN_F + 14, 1), ff(BN_F + 13, 1), AF.Ln), inc=False)
                drain("scalar")
                vacR = acc(lambda en: en.activation(
                    ff(BN_F + 14, 1), ff(BN_F + 14, 1), AF.Exp, scale=-0.5))
                W("vector", "ac", vacR)
                vdvX = dvv(lambda en: en.tensor_scalar(
                    ff(PSTG_F, 1024), ff(PSTG_F, 1024), ff(BN_F + 12, 1),
                    ff(BN_F + 14, 1), ALU.subtract, ALU.mult))
                drain("vector")
                W("scalar", "dv", vdvX)
                vx1 = dma("scalar", "dC", ap["x1_dram"][row0:row0 + P, :],
                          ff(PSTG_F, 1024))
                if debug:
                    dma("scalar", "dC", ap["dbg_x1"][row0:row0 + P, :],
                        ff(PSTG_F, 1024))
                W("vector", "dC", wq_state["obf_free"])
                W("vector", "dC", st2["x1q_free"])
                vq = emit_quant(ff(PSTG_F, 1024), bf(XQR_C, 1024), 1024,
                                ff(QSA_F, 1024), ff(QSB_F, 1024),
                                ff(QSC_F, 1024),
                                ff(SM_F, 64), ff(SM_F + 64, 64),
                                ff(SM_F + 128, 64), ff(SM_F + 192, 64))
                wq_state["stg_free"] = vq
                st2["pstg_free_dv"] = vq
                st2["pstg_free_dc"] = vx1
                W("scalar", "dv", vq)
                vxq = dma("scalar", "dC", ap["xq_dram"][row0:row0 + P, :],
                          bf(XQR_C, 1024))
                if debug:
                    dma("scalar", "dC", ap["dbg_xq"][row0:row0 + P, :],
                        bf(XQR_C, 1024))
                wq_state["obf_free"] = vxq
                st2["x1q_free"] = vxq

        attention_slot(0)
        slot_post(0)
        attention_slot(1)
        slot_post(1)
        while wq_state["i"] < len(wq_tiles):
            emit_wq_step(4)

        for r in range(C // P):
            stg = ff(H0_F, I_FFN)
            obf = bf(WDQ3_C, I_FFN)
            W("sync", "dv", wq_state["stg_free"])
            ld = dma("sync", "dS", stg, ap["wd"][r * P:(r + 1) * P, :])
            W("vector", "dS", ld)
            W("vector", "dC", wq_state["obf_free"])
            vq = emit_quant(stg, obf, I_FFN,
                            ff(Q31_F, I_FFN), ff(Q32_F, I_FFN),
                            ff(Q33_F, I_FFN),
                            ff(SM_F, 176), ff(SM_F + 176, 176),
                            ff(R3A_F, 176), ff(R3B_F, 176))
            wq_state["stg_free"] = vq
            W("scalar", "dv", vq)
            wq_state["obf_free"] = dma("scalar", "dC",
                                       ap["wdq_dram"][r * P:(r + 1) * P, :],
                                       obf)

        W("sync", "dC", sems["dC"].v)
        tstg = bf(WBG0_C, I_FFN)
        vtr = 0
        for src_d, dst_d, n, wdt2 in (("wgq_dram", "wgqT_dram", 8, I_FFN),
                                      ("wuq_dram", "wuqT_dram", 8, I_FFN),
                                      ("wdq_dram", "wdqT_dram", NIC, C)):
            for ccx in range(n):
                W("sync", "dC", vtr if False else 0)
                vl = dma("sync", "dS", tstg[:, 0:wdt2],
                         ap[src_d][:, ccx * P:(ccx + 1) * P], transpose=True)
                W("scalar", "dS", vl)
                vtr = dma("scalar", "dC", ap[dst_d][ccx], tstg[:, 0:wdt2])
                W("sync", "dC", vtr)
        W("sync", "dC", sems["dC"].v)
        v_xqt = 0
        for cc in range(8):
            v_xqt = dma("sync", "dS", bf(XQT_C + cc * 2048, 2048),
                        ap["xq_dram"][:, cc * P:(cc + 1) * P], transpose=True)
        xqt3 = bf(XQT_C, 16384).rearrange("p (a t) -> p a t", a=8)

        IBANDS = [(0, 512), (512, 512), (1024, 512), (1536, 512),
                  (2048, 512), (2560, 256)]
        wb_g = [bf(WBG0_C, 4096), bf(WBG1_C, 4096)]
        wb_u = [bf(WBU0_C, 4096), bf(WBU1_C, 4096)]
        st3 = {"wb_free_pe": [0, 0], "hq_free_dc": 0,
               "gu_free_ac": [0, 0], "gu_free_dv": [0, 0],
               "h_free": [0, 0]}

        def load_band2(ib, wb):
            i0, wdt = IBANDS[ib]
            W("sync", "pe", st3["wb_free_pe"][wb])
            dma("sync", "dS",
                wb_g[wb].rearrange("p (a w) -> p a w", a=8)[:, :, 0:wdt],
                ap["wgqT_dram"][:, :, i0:i0 + wdt].rearrange(
                    "a p w -> p a w"))
            return dma("sync", "dS",
                       wb_u[wb].rearrange("p (a w) -> p a w", a=8)
                       [:, :, 0:wdt],
                       ap["wuqT_dram"][:, :, i0:i0 + wdt].rearrange(
                           "a p w -> p a w"))

        hbufs = (ff(H0_F, I_FFN), ff(H1_F, I_FFN))
        PSG = (PS_SA, PS_SB)
        PSU = (PS_Y0, PS_Y1)
        band_ld = {(0, 0): load_band2(0, 0)}
        W("tensor", "ac", st2["s_free"][0])
        prime()

        for grp in range(8):
            for ib in range(6):
                wb = ib % 2
                i0, wdt = IBANDS[ib]
                ldv = band_ld.pop((grp, ib))
                if ib + 1 < 6:
                    band_ld[(grp, ib + 1)] = load_band2(ib + 1, 1 - wb)
                elif grp + 1 < 8:
                    band_ld[(grp + 1, 0)] = load_band2(0, 1 - wb)
                W("tensor", "dS", ldv)
                if grp == 0 and ib == 0:
                    W("tensor", "dS", v_xqt)
                for ts in range(2):
                    tsub = grp * 2 + ts
                    psg, psu = PSG[ts], PSU[ts]
                    W("tensor", "ac", st3["gu_free_ac"][ts])
                    W("tensor", "dv", st3["gu_free_dv"][ts])
                    vpeGU = None
                    for a in range(8):
                        def mmg(en, a=a, tsub=tsub, wb=wb, wdt=wdt, psg=psg):
                            return en.matmul(
                                psg[:, 0:wdt],
                                xqt3[:, a, tsub * 128:(tsub + 1) * 128],
                                wb_g[wb][:, a * 512:a * 512 + wdt],
                                start=(a == 0), stop=(a == 7))
                        pe(mmg, inc=False)
                        def mmu(en, a=a, tsub=tsub, wb=wb, wdt=wdt, psu=psu):
                            return en.matmul(
                                psu[:, 0:wdt],
                                xqt3[:, a, tsub * 128:(tsub + 1) * 128],
                                wb_u[wb][:, a * 512:a * 512 + wdt],
                                start=(a == 0), stop=(a == 7))
                        if a == 7:
                            vpeGU = pe(mmu)
                        else:
                            pe(mmu, inc=False)
                    st3["wb_free_pe"][wb] = vpeGU
                    W("scalar", "pe", vpeGU)
                    W("scalar", "dv", st3.get("sil_free_dv", 0))
                    vsil = acc(lambda en, psg=psg, wdt=wdt: en.activation(
                        ff(SIL_F, wdt), psg[:, 0:wdt], AF.Silu))
                    st3["gu_free_ac"][ts] = vsil
                    W("vector", "ac", vsil)
                    W("vector", "dv", st3["h_free"][ts])
                    vh = dvv(lambda en, ts=ts, i0=i0, wdt=wdt, psu=psu:
                             en.tensor_tensor(
                                 hbufs[ts][:, i0:i0 + wdt],
                                 ff(SIL_F, wdt), psu[:, 0:wdt], ALU.mult))
                    st3["gu_free_dv"][ts] = vh
                    st3["sil_free_dv"] = vh
                    drain("vector")
            for ts in range(2):
                tsub = grp * 2 + ts
                W("vector", "dC", st3["hq_free_dc"])
                vq = emit_quant(hbufs[ts], bf(HQS_C, I_FFN), I_FFN,
                                ff(Q31_F, I_FFN), ff(Q32_F, I_FFN),
                                ff(Q33_F, I_FFN),
                                ff(SM_F, 176), ff(SM_F + 176, 176),
                                ff(R3A_F, 176), ff(R3B_F, 176))
                st3["h_free"][ts] = vq
                W("scalar", "dv", vq)
                st3["hq_free_dc"] = dma(
                    "scalar", "dC",
                    ap["hq_dram"][tsub * P:(tsub + 1) * P, :],
                    bf(HQS_C, I_FFN))

        W("sync", "dC", sems["dC"].v)
        W("sync", "dv", sems["dv"].v)
        W("sync", "pe", st3["wb_free_pe"][0])
        W("sync", "pe", st3["wb_free_pe"][1])
        vtr = 0
        for icx in range(NIC):
            vl = dma("sync", "dS", bf(WBG0_C, 2048),
                     ap["hq_dram"][:, icx * P:(icx + 1) * P], transpose=True)
            W("scalar", "dS", vl)
            vtr = dma("scalar", "dC", ap["hqT_dram"][icx], bf(WBG0_C, 2048))
            W("sync", "dC", vtr)
        hqt = [bf(HQT0_C, 2048), bf(HQT1_C, 2048)]
        wdt_ = [bf(WDT0_C, 1024), bf(WDT1_C, 1024)]
        st4 = {"ic_free_pe": [0, 0], "ps_free": [0, 0, 0, 0]}
        PSD = (PS_SA, PS_SB, PS_Y0, PS_Y1)

        def load_ic(ic, buf):
            W("sync", "pe", st4["ic_free_pe"][buf])
            dma("sync", "dS", hqt[buf], ap["hqT_dram"][ic])
            return dma("sync", "dS", wdt_[buf], ap["wdqT_dram"][ic])

        ld_next = None
        W("tensor", "ac", st3["gu_free_ac"][0])
        prime()
        for tg in range(4):
            ld_ic = {0: load_ic(0, 0)} if ld_next is None else ld_next
            ld_next = None
            vpeD = None
            for ic in range(NIC):
                buf = ic % 2
                if ic + 1 < NIC:
                    ld_ic[ic + 1] = load_ic(ic + 1, 1 - buf)
                elif tg + 1 < 4:
                    ld_next = {0: load_ic(0, 1 - buf)}
                W("tensor", "dS", ld_ic[ic])
                for ts in range(4):
                    tsub = tg * 4 + ts
                    ps = PSD[ts]
                    if ic == 0:
                        W("tensor", "ac", st4["ps_free"][ts])
                    for half in range(2):
                        def mmd(en, half=half, ic=ic, tsub=tsub, buf=buf,
                                ps=ps):
                            return en.matmul(
                                ps[:, half * 512:(half + 1) * 512],
                                hqt[buf][:, tsub * 128:tsub * 128 + 128],
                                wdt_[buf][:, half * 512:(half + 1) * 512],
                                start=(ic == 0), stop=(ic == NIC - 1))
                        if half == 1 and ts == 3:
                            vpeD = pe(mmd)
                        else:
                            pe(mmd, inc=False)
                st4["ic_free_pe"][buf] = vpeD
            for ts in range(4):
                tsub = tg * 4 + ts
                ps = PSD[ts]
                row0 = tsub * P
                W("scalar", "pe", vpeD)
                W("scalar", "dC", st4.get("pstg3_free_dc", 0))
                acc(lambda en, ps=ps: en.activation(
                    ff(PSTG3_F, 512), ps[:, 0:512], AF.Copy), inc=False)
                vacD = acc(lambda en, ps=ps: en.activation(
                    ff(PSTG3_F + 512, 512), ps[:, 512:1024], AF.Copy))
                st4["ps_free"][ts] = vacD
                W("sync", "dv", st4.get("x1s_free_dv", 0))
                vx1 = dma("sync", "dS", ff(X1S_F, 1024),
                          ap["x1_dram"][row0:row0 + P, :])
                W("vector", "ac", vacD)
                W("vector", "dS", vx1)
                vres2 = dvv(lambda en: en.tensor_tensor(
                    ff(PSTG3_F, 1024), ff(PSTG3_F, 1024), ff(X1S_F, 1024),
                    ALU.add))
                st4["x1s_free_dv"] = vres2
                drain("vector")
                dvv(lambda en: en.bn_stats(ff(BN3_F, 6), ff(PSTG3_F, 512)),
                    inc=False)
                dvv(lambda en: en.bn_stats(ff(BN3_F + 6, 6),
                                           ff(PSTG3_F + 512, 512)),
                    inc=False)
                drain("vector")
                vdvS = dvv(lambda en: en.bn_aggr(ff(BN3_F + 12, 2),
                                                 ff(BN3_F, 12)))
                drain("vector")
                dvv(lambda en: en.tensor_scalar(
                    ff(BN3_F + 13, 1), ff(BN3_F + 13, 1), LNEPS, None,
                    ALU.add), inc=False)
                drain("vector")
                vdvS = dvv(lambda en: en.tensor_scalar(
                    ff(BN3_F + 15, 1), ff(BN3_F + 13, 1), 1.0, None,
                    ALU.mult))
                W("scalar", "dv", vdvS)
                acc(lambda en: en.activation(
                    ff(BN3_F + 14, 1), ff(BN3_F + 13, 1), AF.Ln), inc=False)
                drain("scalar")
                vacR = acc(lambda en: en.activation(
                    ff(BN3_F + 14, 1), ff(BN3_F + 14, 1), AF.Exp,
                    scale=-0.5))
                W("vector", "ac", vacR)
                vdvX = dvv(lambda en: en.tensor_scalar(
                    ff(PSTG3_F, 1024), ff(PSTG3_F, 1024), ff(BN3_F + 12, 1),
                    ff(BN3_F + 14, 1), ALU.subtract, ALU.mult))
                drain("vector")
                W("scalar", "dv", vdvX)
                st4["pstg3_free_dc"] = dma(
                    "scalar", "dC", ap["out"][row0:row0 + P, :],
                    ff(PSTG3_F, 1024))

        W("sync", "dC", sems["dC"].v)
        W("gpsimd", "dC", sems["dC"].v)

        @block.sync
        def _(en):
            for f in ops["sync"]:
                f(en)

        @block.gpsimd
        def _(en):
            for f in ops["gpsimd"]:
                f(en)

        @block.tensor
        def _(en):
            for f in ops["tensor"]:
                f(en)

        @block.vector
        def _(en):
            for f in ops["vector"]:
                f(en)

        @block.scalar
        def _(en):
            for f in ops["scalar"]:
                f(en)

    return nc, ap


def kernel(**inputs):
    in_maps = _host_prepare(inputs)
    nc, _ = build_program(debug=False)
    res = run_bass_kernel_spmd(nc, in_maps, core_ids=list(range(8)))
    outp = np.empty((2, T, C), np.float32)
    for core in range(8):
        b, g = core // 4, core % 4
        o = np.asarray(res.results[core]["out"])
        outp[b, g * SUP:(g + 1) * SUP] = o[0:SUP]
        outp[b, (7 - g) * SUP:(8 - g) * SUP] = o[SUP:2 * SUP]
    return outp


# version 05
# speedup vs baseline: 1.1251x; 1.1251x over previous
import numpy as np
import ml_dtypes
import concourse.bass as bass
import concourse.mybir as mybir
from concourse.bass_utils import run_bass_kernel_spmd

F32 = mybir.dt.float32
BF16 = mybir.dt.bfloat16
I32 = mybir.dt.int32
AF = mybir.ActivationFunctionType
ALU = mybir.AluOpType
AX = mybir.AxisListType

P = 128
T = 8192
C = 1024
NS = 8
SUP = 1024
NPAIR = 8
I_FFN = 2816
COVER = (4, 8)
DIAGPOS = (3, 7)
NTOK = 2048
LNEPS = 1e-5
NIC = I_FFN // P

BF2 = np.dtype(ml_dtypes.bfloat16)

QT_C = 0
LBQ_C = 16384
LBK_C = 17408
LBV_C = 18432
LAT_C = 19456
UBP1_C = 19712
MSK_C = 16384
WKR_C = 24576
WVR_C = 32768
KT0_C = 24576
KT1_C = 32768
YNT_C = 24576
XTB0_C = 40960
XTB1_C = 49280
V0_C = 40960
V1_C = 49280
E_C = 57600
SQB_C = 62720
KSTG0_C = 63744
KSTG1_C = 64768
VSTG0_C = 65792
VSTG1_C = 66832
XQR_C = 65792
ONE_C = 67872
LAPT_C = 67873
LBP_C = 68129
YN_STG_C = 69153
ARB_N = 70400
XQT_C = 0
WBG0_C = 16384
WBG1_C = 20480
WBU0_C = 24576
WBU1_C = 28672
HQS_C = 32768
WDQ3_C = 35584
HQT0_C = 38400
HQT1_C = 40448
WDT0_C = 42496
WDT1_C = 43520

YACC_F = 0
SENT_F = 8320
CNEG_F = 8336
BIAS_F = 8344
KMX_F = 8472
QMX_F = 8536
KM2_F = 8552
QM2_F = 8560
CSC_F = 8568
BN_F = 8576
PSTG_F = 8600
PSTG2_F = 9624
QSA_F = 10648
QSB_F = 11672
QSC_F = 12696
WQS_F = 13720
DUMP_F = 14744
SM_F = 15256
U2B_F = 16384
ARF_N = 16896
RECN_F = SM_F + 280
H0_F = 0
H1_F = 2816
Q31_F = 5632
Q32_F = 8448
Q33_F = 11264
X1S_F = 14080
R3A_F = DUMP_F
R3B_F = DUMP_F + 176
BN3_F = SM_F + 400
PSTG3_F = QSA_F
SIL_F = 15800


def _perm_for_core(g):
    d0, d1 = g, 7 - g
    others = [s for s in range(8) if s not in (d0, d1)]
    first = [s for s in others if s < d0]
    rest = [s for s in others if s >= d0]
    p012 = (first + rest)[:3]
    remaining = [s for s in others if s not in p012]
    perm = p012 + [d0] + remaining + [d1]
    assert len(perm) == 8 and sorted(perm) == list(range(8))
    return perm


def _host_prepare(inp):
    x = np.asarray(inp["L0_attn_x"], np.float32)
    w_attn = np.asarray(inp["L0_attn_w_attn"], np.float32)
    b_attn = np.asarray(inp["L0_attn_b_attn"], np.float32)
    la = np.asarray(inp["L0_attn_la_attn"], np.float32)
    lb = np.asarray(inp["L0_attn_lb_attn"], np.float32)
    w_proj = np.asarray(inp["L0_attn_w_proj"], np.float32)
    b_proj = np.asarray(inp["L0_attn_b_proj"], np.float32)
    lap = np.asarray(inp["L0_attn_la_proj"], np.float32)
    lbp = np.asarray(inp["L0_attn_lb_proj"], np.float32)

    shared = {
        "wqT": np.ascontiguousarray(w_attn[0:1024].T),
        "wkT": np.ascontiguousarray(w_attn[1024:2048].T),
        "wvT": np.ascontiguousarray(w_attn[2048:3072].T),
        "laT": np.ascontiguousarray(la.T),
        "lbqA": np.ascontiguousarray(
            np.concatenate([lb[0:1024].T, b_attn[None, 0:1024]], 0)),
        "lbkA": np.ascontiguousarray(
            np.concatenate([lb[1024:2048].T, b_attn[None, 1024:2048]], 0)),
        "lbvA": np.ascontiguousarray(
            np.concatenate([lb[2048:3072].T, b_attn[None, 2048:3072]], 0)),
        "wpT": np.ascontiguousarray(w_proj.T),
        "lapT": np.ascontiguousarray(lap.T),
        "lbpA": np.ascontiguousarray(
            np.concatenate([lbp.T, b_proj[None, :]], 0)),
        "wg": np.asarray(inp["L0_ffn_w_gate"], np.float32),
        "wu": np.asarray(inp["L0_ffn_w_up"], np.float32),
        "wd": np.asarray(inp["L0_ffn_w_down"], np.float32),
    }
    kk = np.arange(P)[:, None]
    qq = np.arange(SUP)[None, :]
    m = np.empty((P, NS * SUP), np.float32)
    for j in range(8):
        m[:, j * SUP:(j + 1) * SUP] = ((j * P + kk) <= qq)
    shared["dmask"] = m.astype(BF2)

    in_maps = []
    for core in range(8):
        b, g = core // 4, core % 4
        perm = _perm_for_core(g)
        xT = x[b].T
        d = dict(shared)
        d["xTperm"] = np.ascontiguousarray(
            xT.reshape(C, NS, SUP)[:, perm, :].reshape(C, T))
        own = [g, 7 - g]
        d["xTown"] = np.ascontiguousarray(
            xT.reshape(C, NS, SUP)[:, own, :].reshape(C, NTOK))
        d["xres"] = np.ascontiguousarray(
            x[b].reshape(NS, SUP, C)[own].reshape(NTOK, C))
        sent = np.full((P, 16), -1.0e30, np.float32)
        for slot in range(2):
            qs = own[slot]
            for pos in range(COVER[slot]):
                s_at = perm[pos]
                valid = (s_at < qs) or (pos == DIAGPOS[slot])
                sent[:, slot * 8 + pos] = 1.0e30 if valid else -1.0e30
        d["sent"] = sent
        in_maps.append(d)
    return in_maps


class _Sem:
    def __init__(self, handle):
        self.h = handle
        self.v = 0


def build_program(debug=False):
    nc = bass.Bass("TRN2", target_bir_lowering=False, debug=False)
    ap = {}

    def din(name, shape, dt):
        ap[name] = nc.dram_tensor(name, shape, dt, kind="ExternalInput").ap()

    def dint(name, shape, dt):
        ap[name] = nc.dram_tensor(name, shape, dt, kind="Internal").ap()

    din("xTperm", [C, T], F32)
    din("xTown", [C, NTOK], F32)
    din("xres", [NTOK, C], F32)
    din("wqT", [C, C], F32)
    din("wkT", [C, C], F32)
    din("wvT", [C, C], F32)
    din("laT", [C, 32], F32)
    din("lbqA", [33, C], F32)
    din("lbkA", [33, C], F32)
    din("lbvA", [33, C], F32)
    din("wpT", [C, C], F32)
    din("lapT", [C, 32], F32)
    din("lbpA", [33, C], F32)
    din("wg", [I_FFN, C], F32)
    din("wu", [I_FFN, C], F32)
    din("wd", [C, I_FFN], F32)
    din("dmask", [P, NS * SUP], BF16)
    din("sent", [P, 16], F32)

    dint("kt_dram", [NS, NPAIR, P, SUP], BF16)
    dint("v_dram", [NS, 8, P, 16 * 65], BF16)
    dint("cneg_dram", [1, NPAIR], F32)
    dint("x1_dram", [NTOK, C], F32)
    dint("xq_dram", [NTOK, C], BF16)
    dint("ynorm_dram", [2, SUP, C], BF16)
    dint("hq_dram", [NTOK, I_FFN], BF16)
    dint("wgq_dram", [I_FFN, C], BF16)
    dint("wuq_dram", [I_FFN, C], BF16)
    dint("wdq_dram", [C, I_FFN], BF16)
    dint("wgqT_dram", [8, P, I_FFN], BF16)
    dint("wuqT_dram", [8, P, I_FFN], BF16)
    dint("wdqT_dram", [NIC, P, C], BF16)
    dint("hqT_dram", [NIC, P, NTOK], BF16)

    ap["out"] = nc.dram_tensor("out", [NTOK, C], F32,
                               kind="ExternalOutput").ap()
    if debug:
        for nm, shp, dt in [("dbg_p", [NTOK, C], F32),
                            ("dbg_x1", [NTOK, C], F32),
                            ("dbg_xq", [NTOK, C], BF16),
                            ("dbg_cneg", [1, NPAIR], F32),
                            ("dbg_yn", [2, SUP, C], BF16)]:
            ap[nm] = nc.dram_tensor(nm, shp, dt, kind="ExternalOutput").ap()

    from contextlib import ExitStack
    with ExitStack() as ctx:
        e = ctx.enter_context
        ARB = e(nc.sbuf_tensor("ARB", [P, ARB_N], BF16))
        ARF = e(nc.sbuf_tensor("ARF", [P, ARF_N], F32))

        def bf(c0, n, p0=0, prows=P):
            return ARB[p0:p0 + prows, c0:c0 + n]

        def ff(c0, n, p0=0, prows=P):
            return ARF[p0:p0 + prows, c0:c0 + n]

        PS_SA = e(nc.psum_tensor("PS_SA", [P, 1024], F32))
        PS_SB = e(nc.psum_tensor("PS_SB", [P, 1024], F32))
        PS_Y0 = e(nc.psum_tensor("PS_Y0", [P, 1024], F32))
        PS_Y1 = e(nc.psum_tensor("PS_Y1", [P, 1024], F32))
        PSY = (PS_Y0, PS_Y1)

        sems = {}
        for name in ("dS", "dG", "dC", "pe", "ac", "dv", "gp"):
            sems[name] = _Sem(e(nc.semaphore(name)))
        block = e(nc.Block())

        ops = {k: [] for k in ("sync", "gpsimd", "tensor", "vector", "scalar")}

        def W(eng, sem, val):
            if val is None or val <= 0:
                return
            h = sems[sem].h
            ops[eng].append(lambda en: en.wait_ge(h, val))

        def dma(eng, sem, outap, inap, transpose=False):
            s = sems[sem]
            s.v += 16
            h = s.h
            if transpose:
                ops[eng].append(
                    lambda en: en.dma_start_transpose(outap, inap)
                    .then_inc(h, 16))
            else:
                ops[eng].append(
                    lambda en: en.dma_start(out=outap, in_=inap)
                    .then_inc(h, 16))
            return s.v

        def _mk(eng, sem):
            def run(fn, inc=True):
                if inc:
                    s = sems[sem]
                    s.v += 1
                    h = s.h
                    ops[eng].append(lambda en: fn(en).then_inc(h, 1))
                    return s.v
                ops[eng].append(fn)
                return None
            return run

        pe = _mk("tensor", "pe")
        acc = _mk("scalar", "ac")
        dvv = _mk("vector", "dv")
        gpc = _mk("gpsimd", "gp")

        def drain(eng):
            ops[eng].append(lambda en: en.drain())

        v_sent = dma("sync", "dS", ff(SENT_F, 16), ap["sent"][:, :])
        dma("gpsimd", "dG", bf(LAPT_C, 256).rearrange("p (a n) -> p a n", a=8),
            ap["lapT"].rearrange("(a p) r -> p a r", p=P))
        dma("gpsimd", "dG", bf(LBP_C, 1024, 0, 33), ap["lbpA"][:, :])
        dma("gpsimd", "dG", bf(LBQ_C, 1024, 0, 33), ap["lbqA"][:, :])
        dma("gpsimd", "dG", bf(LBK_C, 1024, 0, 33), ap["lbkA"][:, :])
        dma("gpsimd", "dG", bf(LBV_C, 1024, 0, 33), ap["lbvA"][:, :])
        dma("gpsimd", "dG", bf(LAT_C, 256).rearrange("p (a n) -> p a n", a=8),
            ap["laT"].rearrange("(a p) r -> p a r", p=P))
        gpc(lambda en: en.memset(bf(ONE_C, 1), 1.0), inc=False)
        gpc(lambda en: en.memset(bf(UBP1_C, 1024, 32, 1), 1.0), inc=False)
        gpc(lambda en: en.memset(
            ARF[32:33, U2B_F:U2B_F + 512].bitcast(BF16), 1.0), inc=False)
        gpc(lambda en: en.memset(bf(VSTG0_C, 1040).rearrange(
            "p (h w) -> p h w", w=65)[:, :, 64:65], 1.0), inc=False)
        v_ms = gpc(lambda en: en.memset(bf(VSTG1_C, 1040).rearrange(
            "p (h w) -> p h w", w=65)[:, :, 64:65], 1.0))
        v_wk = dma("gpsimd", "dG", bf(WKR_C, 8192).rearrange("p (a n) -> p a n", a=8),
                   ap["wkT"].rearrange("(a p) o -> p a o", p=P))
        v_wv = dma("gpsimd", "dG", bf(WVR_C, 8192).rearrange("p (a n) -> p a n", a=8),
                   ap["wvT"].rearrange("(a p) o -> p a o", p=P))

        xtb = [bf(XTB0_C, 8192), bf(XTB1_C, 8192)]
        kstg = [bf(KSTG0_C, 1024), bf(KSTG1_C, 1024)]
        vstg = [bf(VSTG0_C, 1040), bf(VSTG1_C, 1040)]
        sqb_b = [bf(E_C, 1024), bf(E_C + 1024, 1024)]
        ub = bf(UBP1_C, 1024, 0, 33)
        u2b = ARF[0:33, U2B_F:U2B_F + 512].bitcast(BF16)
        wkr = bf(WKR_C, 8192).rearrange("p (a o) -> p a o", a=8)
        wvr = bf(WVR_C, 8192).rearrange("p (a o) -> p a o", a=8)
        laTb = bf(LAT_C, 256).rearrange("p (a r) -> p a r", a=8)
        lbq = bf(LBQ_C, 1024, 0, 33)
        lbk = bf(LBK_C, 1024, 0, 33)
        lbv = bf(LBV_C, 1024, 0, 33)
        lbp = bf(LBP_C, 1024, 0, 33)
        onec = bf(ONE_C, 1)

        def xsrc(pos):
            return ap["xTperm"][:, pos * SUP:(pos + 1) * SUP].rearrange(
                "(a p) t -> p a t", p=P)

        wq_tiles = []
        for wname, dst in (("wg", "wgq_dram"), ("wu", "wuq_dram")):
            for r in range(NIC):
                wq_tiles.append((ap[wname][r * P:(r + 1) * P, :],
                                 ap[dst][r * P:(r + 1) * P, :]))
        wq_state = {"i": 0, "stg_free": 0, "obf_free": 0, "ld": 0}

        def emit_quant(src, dst, width, s1, s2, s3, mblk, sblk, rblk, s2blk):
            nb = width // 16
            x3 = src.rearrange("p (b k) -> p b k", k=16)
            dvv(lambda en: en.tensor_reduce(mblk, x3, AX.X, ALU.max,
                                            apply_absolute_value=True),
                inc=False)
            drain("vector")
            dvv(lambda en: en.tensor_scalar(sblk, mblk, 1.0 / 6.0, 1e-12,
                                            ALU.mult, ALU.max), inc=False)
            drain("vector")
            dvv(lambda en: en.reciprocal(rblk, sblk), inc=False)
            dvv(lambda en: en.tensor_scalar(s2blk, sblk, 2.0, None, ALU.mult),
                inc=False)
            drain("vector")
            dvv(lambda en: en.tensor_tensor(
                s1.rearrange("p (b k) -> p b k", k=16), x3,
                rblk.unsqueeze(-1).broadcast_to([P, nb, 16]), ALU.mult),
                inc=False)
            drain("vector")
            dvv(lambda en: en.tensor_scalar(
                s2.bitcast(I32), s1.bitcast(I32), 0x7FFFFFFF, None,
                ALU.bitwise_and), inc=False)
            drain("vector")
            dvv(lambda en: en.tensor_scalar(s3, s2, 1.0, None, ALU.max),
                inc=False)
            drain("vector")
            dvv(lambda en: en.tensor_scalar(
                s3.bitcast(I32), s3.bitcast(I32), 8, None,
                ALU.logical_shift_right), inc=False)
            drain("vector")
            dvv(lambda en: en.tensor_scalar(
                s3.bitcast(I32), s3.bitcast(I32), 8192.0, None, ALU.add),
                inc=False)
            drain("vector")
            dvv(lambda en: en.tensor_scalar(
                s3.bitcast(I32), s3.bitcast(I32), 0xFFC000, 8,
                ALU.bitwise_and, ALU.logical_shift_left), inc=False)
            drain("vector")
            dvv(lambda en: en.tensor_scalar(
                s1, s2, 0.25, -0.5, ALU.is_le, ALU.mult), inc=False)
            drain("vector")
            dvv(lambda en: en.tensor_scalar(
                s2, s2, 0.75, -0.5, ALU.is_le, ALU.mult), inc=False)
            drain("vector")
            dvv(lambda en: en.tensor_tensor(s1, s1, s2, ALU.add), inc=False)
            drain("vector")
            dvv(lambda en: en.tensor_tensor(s3, s3, s1, ALU.add), inc=False)
            dvv(lambda en: en.tensor_scalar(
                s2, src, 0.0, -0.5, ALU.is_ge, ALU.add), inc=False)
            drain("vector")
            dvv(lambda en: en.tensor_tensor(
                s2.rearrange("p (b k) -> p b k", k=16),
                s2.rearrange("p (b k) -> p b k", k=16),
                s2blk.unsqueeze(-1).broadcast_to([P, nb, 16]), ALU.mult),
                inc=False)
            drain("vector")
            v = dvv(lambda en: en.tensor_tensor(dst, s3, s2, ALU.mult))
            drain("vector")
            return v

        wq_pending = []

        def _wq_build_tile(obf_c):
            i = wq_state["i"]
            if i >= len(wq_tiles):
                return False
            wq_state["i"] = i + 1
            src, dst = wq_tiles[i]
            stg = ff(WQS_F, 1024)
            obf = bf(obf_c, 1024)
            s1, s2, s3 = ff(QSA_F, 1024), ff(QSB_F, 1024), ff(QSC_F, 1024)
            mblk, sblk = ff(SM_F, 64), ff(SM_F + 64, 64)
            rblk, s2blk = ff(SM_F + 128, 64), ff(SM_F + 192, 64)
            x3 = stg.rearrange("p (b k) -> p b k", k=16)

            def t_load():
                W("sync", "dv", wq_state["stg_free"])
                wq_state["ld"] = dma("sync", "dS", stg, src)

            def t0():
                W("vector", "dS", wq_state["ld"])
                dvv(lambda en: en.tensor_reduce(mblk, x3, AX.X, ALU.max,
                                                apply_absolute_value=True),
                    inc=False)
                drain("vector")

            def t1():
                dvv(lambda en: en.tensor_scalar(sblk, mblk, 1.0 / 6.0, 1e-12,
                                                ALU.mult, ALU.max), inc=False)
                drain("vector")

            def t2():
                dvv(lambda en: en.reciprocal(rblk, sblk), inc=False)
                dvv(lambda en: en.tensor_scalar(s2blk, sblk, 2.0, None,
                                                ALU.mult), inc=False)
                drain("vector")

            def t3():
                dvv(lambda en: en.tensor_tensor(
                    s1.rearrange("p (b k) -> p b k", k=16), x3,
                    rblk.unsqueeze(-1).broadcast_to([P, 64, 16]), ALU.mult),
                    inc=False)
                drain("vector")

            def t4():
                dvv(lambda en: en.tensor_scalar(
                    s2.bitcast(I32), s1.bitcast(I32), 0x7FFFFFFF, None,
                    ALU.bitwise_and), inc=False)
                drain("vector")

            def t5():
                dvv(lambda en: en.tensor_scalar(s3, s2, 1.0, None, ALU.max),
                    inc=False)
                drain("vector")

            def t6():
                dvv(lambda en: en.tensor_scalar(
                    s3.bitcast(I32), s3.bitcast(I32), 8, None,
                    ALU.logical_shift_right), inc=False)
                drain("vector")

            def t7():
                dvv(lambda en: en.tensor_scalar(
                    s3.bitcast(I32), s3.bitcast(I32), 8192.0, None, ALU.add),
                    inc=False)
                drain("vector")

            def t8():
                dvv(lambda en: en.tensor_scalar(
                    s3.bitcast(I32), s3.bitcast(I32), 0xFFC000, 8,
                    ALU.bitwise_and, ALU.logical_shift_left), inc=False)
                drain("vector")

            def t9():
                dvv(lambda en: en.tensor_scalar(
                    s1, s2, 0.25, -0.5, ALU.is_le, ALU.mult), inc=False)
                drain("vector")

            def t10():
                dvv(lambda en: en.tensor_scalar(
                    s2, s2, 0.75, -0.5, ALU.is_le, ALU.mult), inc=False)
                drain("vector")

            def t11():
                dvv(lambda en: en.tensor_tensor(s1, s1, s2, ALU.add),
                    inc=False)
                drain("vector")

            def t12():
                dvv(lambda en: en.tensor_tensor(s3, s3, s1, ALU.add),
                    inc=False)
                dvv(lambda en: en.tensor_scalar(
                    s2, stg, 0.0, -0.5, ALU.is_ge, ALU.add), inc=False)
                drain("vector")

            def t13():
                dvv(lambda en: en.tensor_tensor(
                    s2.rearrange("p (b k) -> p b k", k=16),
                    s2.rearrange("p (b k) -> p b k", k=16),
                    s2blk.unsqueeze(-1).broadcast_to([P, 64, 16]), ALU.mult),
                    inc=False)
                drain("vector")

            def t14():
                W("vector", "dC", wq_state["obf_free"])
                vq = dvv(lambda en: en.tensor_tensor(obf, s3, s2, ALU.mult))
                drain("vector")
                wq_state["stg_free"] = vq
                W("scalar", "dv", vq)
                wq_state["obf_free"] = dma("scalar", "dC", dst, obf)

            wq_pending.extend([t_load, t0, t1, t2, t3, t4, t5, t6, t7, t8,
                               t9, t10, t11, t12, t13, t14])
            return True

        def emit_wq_pump(nstages, obf_c):
            for _ in range(nstages):
                if not wq_pending:
                    if not _wq_build_tile(obf_c):
                        return
                wq_pending.pop(0)()

        st = {
            "ub_free_pe": 0,
            "ulora_ps_free_ac": 0,
            "sa_free_ac": [0, 0],
            "kstg_free_dc": [0, 0],
            "vstg_free_dc": [0, 0],
            "norm_ps_free_dv": 0,
            "sqb_free_pe": [0, 0],
            "xtb_last_pe": [0, 0],
            "w_last_pe": 0,
        }

        def emit_u_lora(xtbuf):
            xt3 = xtbuf.rearrange("p (a t) -> p a t", a=8)
            W("tensor", "ac", st["ulora_ps_free_ac"])
            vpe = None
            for half in range(2):
                for a in range(8):
                    def mm(en, a=a, half=half, xt3=xt3):
                        return en.matmul(
                            PS_Y0[0:32, half * 512:(half + 1) * 512],
                            laTb[:, a, :],
                            xt3[:, a, half * 512:(half + 1) * 512],
                            start=(a == 0), stop=(a == 7))
                    if half == 1 and a == 7:
                        vpe = pe(mm)
                    else:
                        pe(mm, inc=False)
            W("scalar", "pe", vpe)
            W("scalar", "pe", st["ub_free_pe"])
            acc(lambda en: en.activation(ub[0:32, 0:512],
                                         PS_Y0[0:32, 0:512], AF.Copy,
                                         scale=0.5), inc=False)
            vac = acc(lambda en: en.activation(ub[0:32, 512:1024],
                                               PS_Y0[0:32, 512:1024], AF.Copy,
                                               scale=0.5))
            st["ulora_ps_free_ac"] = vac
            return vac, vpe

        def emit_norms(vac_sq, col, sqb):
            W("tensor", "ac", vac_sq)
            W("tensor", "dv", st["norm_ps_free_dv"])
            pe(lambda en: en.matmul(PS_Y1[0:1, 0:512], onec,
                                    sqb[:, 0:512], start=True, stop=True),
               inc=False)
            vpeN = pe(lambda en: en.matmul(PS_Y1[0:1, 512:1024], onec,
                                           sqb[:, 512:1024],
                                           start=True, stop=True))
            st["sqb_free_pe"][0 if sqb is sqb_b[0] else 1] = vpeN
            W("vector", "pe", vpeN)
            dst = ff(col, 1, 0, 1)
            dvv(lambda en: en.tensor_scalar(
                ff(DUMP_F, 512, 0, 1), PS_Y1[0:1, 0:512], 1.0, None,
                ALU.mult, ALU.max, accum_out=dst), inc=False)
            drain("vector")
            vdvN = dvv(lambda en: en.tensor_scalar(
                ff(DUMP_F, 512, 0, 1), PS_Y1[0:1, 512:1024], 1.0, dst,
                ALU.mult, ALU.max, accum_out=dst))
            drain("vector")
            st["norm_ps_free_dv"] = vdvN

        ld_x = [dma("gpsimd", "dG", xtb[0].rearrange("p (a n) -> p a n", a=8), xsrc(0)), 0]
        W("tensor", "dG", v_wv)

        for pos in range(NS):
            buf = pos % 2
            if pos + 1 < NS:
                W("gpsimd", "pe", st["xtb_last_pe"][1 - buf])
                ld_x[1 - buf] = dma(
                    "gpsimd", "dG",
                    xtb[1 - buf].rearrange("p (a n) -> p a n", a=8),
                    xsrc(pos + 1))
            W("tensor", "dG", ld_x[buf])
            xt3 = xtb[buf].rearrange("p (a t) -> p a t", a=8)
            vac_u, _ = emit_u_lora(xtb[buf])
            W("tensor", "ac", vac_u)

            norms_pend = None
            for pair in range(NPAIR):
                sa = pair % 2
                ps = PS_SA if sa == 0 else PS_SB
                W("tensor", "ac", st["sa_free_ac"][sa])
                vpeK = None
                for half in range(2):
                    for a in range(8):
                        def mm(en, a=a, half=half, ps=ps, pair=pair, xt3=xt3):
                            return en.matmul(
                                ps[:, half * 512:(half + 1) * 512],
                                wkr[:, a, pair * 128:(pair + 1) * 128],
                                xt3[:, a, half * 512:(half + 1) * 512],
                                start=(a == 0), stop=False)
                        pe(mm, inc=False)
                    def mml(en, half=half, ps=ps, pair=pair):
                        return en.matmul(
                            ps[:, half * 512:(half + 1) * 512],
                            lbk[:, pair * 128:(pair + 1) * 128],
                            ub[:, half * 512:(half + 1) * 512],
                            start=False, stop=True)
                    if half == 1:
                        vpeK = pe(mml)
                    else:
                        pe(mml, inc=False)
                st["ub_free_pe"] = vpeK
                st["xtb_last_pe"][buf] = vpeK
                st["w_last_pe"] = vpeK
                kb = pair % 2
                W("scalar", "pe", vpeK)
                W("scalar", "dC", st["kstg_free_dc"][kb])
                acc(lambda en, ps=ps, kb=kb: en.activation(
                    kstg[kb][:, 0:512], ps[:, 0:512], AF.Copy), inc=False)
                acc(lambda en, ps=ps, kb=kb: en.activation(
                    kstg[kb][:, 512:1024], ps[:, 512:1024], AF.Copy),
                    inc=False)
                sq = sqb_b[pair % 2]
                W("scalar", "pe", st["sqb_free_pe"][pair % 2])
                acc(lambda en, ps=ps, sq=sq: en.activation(
                    sq[:, 0:512], ps[:, 0:512], AF.Square), inc=False)
                vacS = acc(lambda en, ps=ps, sq=sq: en.activation(
                    sq[:, 512:1024], ps[:, 512:1024], AF.Square))
                st["sa_free_ac"][sa] = vacS
                st["kstg_free_dc"][kb] = dma(
                    "scalar", "dC", ap["kt_dram"][pos, pair, :, :],
                    kstg[kb][:, :])
                if norms_pend is not None:
                    emit_norms(*norms_pend)
                norms_pend = (vacS, KMX_F + pair * 8 + pos, sq)
                if pair == NPAIR - 1:
                    emit_norms(*norms_pend)
                emit_wq_pump(3, E_C + 2 * 1024)

            for tsub in range(8):
                sa = tsub % 2
                ps = PS_SA if sa == 0 else PS_SB
                W("tensor", "ac", st["sa_free_ac"][sa])
                vpeV = None
                for half in range(2):
                    for a in range(8):
                        def mm(en, a=a, half=half, ps=ps, tsub=tsub, xt3=xt3):
                            return en.matmul(
                                ps[:, half * 512:(half + 1) * 512],
                                xt3[:, a, tsub * 128:(tsub + 1) * 128],
                                wvr[:, a, half * 512:(half + 1) * 512],
                                start=(a == 0), stop=False)
                        pe(mm, inc=False)
                    def mml(en, half=half, ps=ps, tsub=tsub):
                        return en.matmul(
                            ps[:, half * 512:(half + 1) * 512],
                            ub[:, tsub * 128:(tsub + 1) * 128],
                            lbv[:, half * 512:(half + 1) * 512],
                            start=False, stop=True)
                    if half == 1:
                        vpeV = pe(mml)
                    else:
                        pe(mml, inc=False)
                st["ub_free_pe"] = vpeV
                st["xtb_last_pe"][buf] = vpeV
                vb = tsub % 2
                W("scalar", "pe", vpeV)
                W("scalar", "dC", st["vstg_free_dc"][vb])
                if pos == 0 and tsub < 2:
                    W("scalar", "gp", v_ms)
                vs3 = vstg[vb].rearrange("p (h w) -> p h w", w=65)
                acc(lambda en, ps=ps, vs3=vs3: en.activation(
                    vs3[:, 0:8, 0:64],
                    ps[:, 0:512].rearrange("p (h d) -> p h d", d=64),
                    AF.Copy), inc=False)
                vacV = acc(lambda en, ps=ps, vs3=vs3: en.activation(
                    vs3[:, 8:16, 0:64],
                    ps[:, 512:1024].rearrange("p (h d) -> p h d", d=64),
                    AF.Copy))
                st["sa_free_ac"][sa] = vacV
                st["vstg_free_dc"][vb] = dma(
                    "scalar", "dC", ap["v_dram"][pos, tsub, :, :],
                    vstg[vb][:, :])
                emit_wq_pump(3, E_C + 2 * 1024)

        W("gpsimd", "pe", st["w_last_pe"])
        v_wq = dma("gpsimd", "dG", bf(WKR_C, 8192).rearrange("p (a n) -> p a n", a=8),
                   ap["wqT"].rearrange("(a p) o -> p a o", p=P))
        wqr = wkr
        for os_ in range(2):
            W("gpsimd", "pe", st["xtb_last_pe"][os_])
            ld = dma("gpsimd", "dG",
                     xtb[os_].rearrange("p (a n) -> p a n", a=8),
                     ap["xTown"][:, os_ * SUP:(os_ + 1) * SUP]
                     .rearrange("(a p) t -> p a t", p=P))
            if os_ == 0:
                W("tensor", "dG", v_wq)
            W("tensor", "dG", ld)
            xt3 = xtb[os_].rearrange("p (a t) -> p a t", a=8)
            vac_u, _ = emit_u_lora(xtb[os_])
            W("tensor", "ac", vac_u)
            norms_pend = None
            for pair in range(NPAIR):
                sa = pair % 2
                ps = PS_SA if sa == 0 else PS_SB
                W("tensor", "ac", st["sa_free_ac"][sa])
                vpeQ = None
                for half in range(2):
                    for a in range(8):
                        def mm(en, a=a, half=half, ps=ps, pair=pair, xt3=xt3):
                            return en.matmul(
                                ps[:, half * 512:(half + 1) * 512],
                                wqr[:, a, pair * 128:(pair + 1) * 128],
                                xt3[:, a, half * 512:(half + 1) * 512],
                                start=(a == 0), stop=False)
                        pe(mm, inc=False)
                    def mml(en, half=half, ps=ps, pair=pair):
                        return en.matmul(
                            ps[:, half * 512:(half + 1) * 512],
                            lbq[:, pair * 128:(pair + 1) * 128],
                            ub[:, half * 512:(half + 1) * 512],
                            start=False, stop=True)
                    if half == 1:
                        vpeQ = pe(mml)
                    else:
                        pe(mml, inc=False)
                st["ub_free_pe"] = vpeQ
                st["xtb_last_pe"][os_] = vpeQ
                W("scalar", "pe", vpeQ)
                qtdst = bf(QT_C + pair * 2048 + os_ * 1024, 1024)
                acc(lambda en, ps=ps, qtdst=qtdst: en.activation(
                    qtdst[:, 0:512], ps[:, 0:512], AF.Copy), inc=False)
                acc(lambda en, ps=ps, qtdst=qtdst: en.activation(
                    qtdst[:, 512:1024], ps[:, 512:1024], AF.Copy), inc=False)
                sq = sqb_b[pair % 2]
                W("scalar", "pe", st["sqb_free_pe"][pair % 2])
                acc(lambda en, ps=ps, sq=sq: en.activation(
                    sq[:, 0:512], ps[:, 0:512], AF.Square), inc=False)
                vacS = acc(lambda en, ps=ps, sq=sq: en.activation(
                    sq[:, 512:1024], ps[:, 512:1024], AF.Square))
                st["sa_free_ac"][sa] = vacS
                if norms_pend is not None:
                    emit_norms(*norms_pend)
                norms_pend = (vacS, QMX_F + pair * 2 + os_, sq)
                if pair == NPAIR - 1:
                    emit_norms(*norms_pend)

        while wq_pending:
            emit_wq_pump(16, E_C + 2 * 1024)
        drain("vector")
        for pair in range(NPAIR):
            dvv(lambda en, pair=pair: en.tensor_reduce(
                ff(KM2_F + pair, 1, 0, 1), ff(KMX_F + pair * 8, 8, 0, 1),
                AX.X, ALU.max), inc=False)
            dvv(lambda en, pair=pair: en.tensor_reduce(
                ff(QM2_F + pair, 1, 0, 1), ff(QMX_F + pair * 2, 2, 0, 1),
                AX.X, ALU.max), inc=False)
        drain("vector")
        vdv = dvv(lambda en: en.tensor_tensor(
            ff(CSC_F, 8, 0, 1), ff(KM2_F, 8, 0, 1), ff(QM2_F, 8, 0, 1),
            ALU.mult))
        W("scalar", "dv", vdv)
        acc(lambda en: en.activation(ff(CSC_F, 8, 0, 1),
                                     ff(CSC_F, 8, 0, 1), AF.Ln), inc=False)
        drain("scalar")
        vac = acc(lambda en: en.activation(
            ff(CSC_F, 8, 0, 1), ff(CSC_F, 8, 0, 1), AF.Exp, scale=0.5))
        W("vector", "ac", vac)
        drain("vector")
        vdv = dvv(lambda en: en.tensor_scalar(
            ff(CSC_F, 8, 0, 1), ff(CSC_F, 8, 0, 1), -0.126256, None,
            ALU.mult))
        drain("vector")
        W("scalar", "dv", vdv)
        vst = dma("scalar", "dC", ap["cneg_dram"][:, :], ff(CSC_F, 8, 0, 1))
        if debug:
            dma("scalar", "dC", ap["dbg_cneg"][:, :], ff(CSC_F, 8, 0, 1))
        W("sync", "dC", vst)
        vld = dma("sync", "dS", ff(CNEG_F, 8),
                  ap["cneg_dram"][0:1, :].broadcast_to([P, 8]))
        W("vector", "dS", vld)
        W("vector", "dS", v_sent)
        for slot in range(2):
            for pos in range(COVER[slot]):
                for pair in range(NPAIR):
                    idx = BIAS_F + slot * 64 + pos * 8 + pair
                    dvv(lambda en, idx=idx, pair=pair, slot=slot, pos=pos:
                        en.tensor_scalar(
                            ff(idx, 1), ff(CNEG_F + pair, 1),
                            ff(SENT_F + slot * 8 + pos, 1), None, ALU.min),
                        inc=False)
        drain("vector")
        vdv_bias = dvv(lambda en: en.tensor_scalar(
            ff(SM_F + 500, 1), ff(BIAS_F, 1), 1.0, None, ALU.mult))

        W("sync", "pe", st["ub_free_pe"])
        v_msk = dma("sync", "dS", bf(MSK_C, 8192), ap["dmask"][:, :])
        W("sync", "dC", sems["dC"].v)

        ktb = [bf(KT0_C, 8192), bf(KT1_C, 8192)]
        vbuf = [bf(V0_C, 8320), bf(V1_C, 8320)]
        Ebuf = [[bf(E_C + i * 1024, 1024) for i in range(3)],
                [bf(E_C + (3 + i) * 1024, 1024) for i in range(3)]]

        def qtv(pair, slot, r0, r1):
            return bf(QT_C + pair * 2048 + slot * 1024, 1024, r0, r1 - r0)

        def prime(n=24):
            for i in range(n):
                def mm(en, i=i):
                    return en.matmul(PS_SA[:, 0:512], bf(QT_C, 128),
                                     bf(QT_C, 512), start=True, stop=True)
                pe(mm, inc=False)
            return pe(lambda en: en.matmul(PS_SA[:, 0:512], bf(QT_C, 128),
                                           bf(QT_C, 512), start=True,
                                           stop=True))

        st2 = {
            "eb": 0,
            "e_free_pe": [[0, 0, 0], [0, 0, 0]],
            "s_free": [st["sa_free_ac"][0], st["sa_free_ac"][1]],
            "psY_evac_dv": st["norm_ps_free_dv"],
            "psY0_ac": st["ulora_ps_free_ac"],
            "x1q_free": 0,
            "kt_last_pe": [0, 0],
            "proj_last_pe": 0,
        }
        kv_ld = {}

        def load_kv(pos, buf):
            W("sync", "pe", st2["kt_last_pe"][buf])
            dma("sync", "dS", ktb[buf].rearrange("p (r n) -> p r n", r=8),
                ap["kt_dram"][pos].rearrange("r p n -> p r n"))
            return dma("sync", "dS",
                       vbuf[buf].rearrange("p (s n) -> p s n", s=8),
                       ap["v_dram"][pos].rearrange("s p n -> p s n"))

        kv_ld[(0, 0)] = load_kv(0, 0)
        W("tensor", "dS", v_msk)
        W("scalar", "dC", wq_state["obf_free"])
        W("scalar", "dv", vdv_bias)
        W("tensor", "ac", st2["psY0_ac"])

        def attention_slot(slot):
            cov = COVER[slot]
            W("tensor", "ac", st2["psY0_ac"])
            W("tensor", "ac", st2["s_free"][0])
            vpr = prime()
            W("scalar", "pe", vpr)
            for pos in range(cov):
                buf = pos % 2
                if (slot, pos) not in kv_ld:
                    kv_ld[(slot, pos)] = load_kv(pos, buf)
                if pos + 1 < cov:
                    kv_ld[(slot, pos + 1)] = load_kv(pos + 1, 1 - buf)
                W("tensor", "dS", kv_ld[(slot, pos)])
                kt3 = ktb[buf].rearrange("p (r n) -> p r n", r=8)
                v3 = vbuf[buf].rearrange("p (s n) -> p s n", s=8)
                diag = pos == DIAGPOS[slot]
                vpeAV = None

                def emit_av(ksub, eb, gate, pair, v3, first):
                    if gate[0] is not None:
                        W("tensor", gate[0], gate[1])
                    if first:
                        W("tensor", "dv", st2["psY_evac_dv"])
                    vv = None
                    for qsub in range(8):
                        pst = PSY[qsub // 4]
                        coff = (0, 130, 512, 642)[qsub % 4]
                        fb = (first and qsub % 2 == 0)
                        for h in range(2):
                            def av(en, h=h, qsub=qsub, ksub=ksub, eb=eb,
                                   pst=pst, coff=coff, v3=v3, pair=pair,
                                   fb=fb):
                                return en.matmul(
                                    pst[:, coff + h * 65:coff + h * 65 + 65],
                                    Ebuf[h][eb]
                                    [:, qsub * 128:(qsub + 1) * 128],
                                    v3[:, ksub,
                                       (pair * 2 + h) * 65:
                                       (pair * 2 + h) * 65 + 65],
                                    start=(fb and h == 0), stop=(ksub == 7))
                            if h == 1 and qsub == 7:
                                vv = pe(av)
                            else:
                                pe(av, inc=False)
                    st2["e_free_pe"][0][eb] = vv
                    st2["e_free_pe"][1][eb] = vv
                    return vv

                for pair in range(NPAIR):
                    bias = ff(BIAS_F + slot * 64 + pos * 8 + pair, 1)
                    pend = None
                    for ksub in range(8):
                        eb = st2["eb"]
                        st2["eb"] = (eb + 1) % 3
                        vpeSA = vpeSB = None
                        W("tensor", "ac", st2["s_free"][0])
                        for half in range(2):
                            def mmSA(en, half=half, pair=pair, ksub=ksub,
                                     kt3=kt3, slot=slot):
                                return en.matmul(
                                    PS_SA[:, half * 512:(half + 1) * 512],
                                    kt3[0:64, pair,
                                        ksub * 128:(ksub + 1) * 128],
                                    qtv(pair, slot, 0, 64)
                                    [:, half * 512:(half + 1) * 512],
                                    start=True, stop=True,
                                    tile_position=(0, 0))
                            if half == 1:
                                vpeSA = pe(mmSA)
                            else:
                                pe(mmSA, inc=False)
                        W("tensor", "ac", st2["s_free"][1])
                        for half in range(2):
                            def mmSB(en, half=half, pair=pair, ksub=ksub,
                                     kt3=kt3, slot=slot):
                                return en.matmul(
                                    PS_SB[:, half * 512:(half + 1) * 512],
                                    kt3[64:128, pair,
                                        ksub * 128:(ksub + 1) * 128],
                                    qtv(pair, slot, 64, 128)
                                    [:, half * 512:(half + 1) * 512],
                                    start=True, stop=True,
                                    tile_position=(64, 0))
                            if half == 1:
                                vpeSB = pe(mmSB)
                            else:
                                pe(mmSB, inc=False)
                        st2["kt_last_pe"][buf] = vpeSB
                        if pend is not None:
                            vpeAV = emit_av(pend[0], pend[1], pend[2], pair,
                                            v3, pend[0] == 0)
                        W("scalar", "pe", vpeSA)
                        W("scalar", "pe", st2["e_free_pe"][0][eb])
                        vacA = acc(lambda en, eb=eb, bias=bias: en.activation(
                            Ebuf[0][eb][:, :], PS_SA[:, :], AF.Exp,
                            bias=bias, scale=0.125))
                        st2["s_free"][0] = vacA
                        W("scalar", "pe", vpeSB)
                        W("scalar", "pe", st2["e_free_pe"][1][eb])
                        vacB = acc(lambda en, eb=eb, bias=bias: en.activation(
                            Ebuf[1][eb][:, :], PS_SB[:, :], AF.Exp,
                            bias=bias, scale=0.125))
                        st2["s_free"][1] = vacB
                        gate = ("ac", vacB)
                        if diag:
                            W("vector", "ac", vacB)
                            mview = bf(MSK_C + ksub * 1024, 1024)
                            dvv(lambda en, eb=eb, mview=mview:
                                en.tensor_tensor(
                                    Ebuf[0][eb][:, :], Ebuf[0][eb][:, :],
                                    mview, ALU.mult), inc=False)
                            vmask = dvv(lambda en, eb=eb, mview=mview:
                                        en.tensor_tensor(
                                            Ebuf[1][eb][:, :],
                                            Ebuf[1][eb][:, :],
                                            mview, ALU.mult))
                            drain("vector")
                            gate = ("dv", vmask)
                        pend = (ksub, eb, gate)
                    vpeAV = emit_av(pend[0], pend[1], pend[2], pair, v3,
                                    False)
                    W("vector", "pe", vpeAV)
                    vdvY = None
                    for t in range(2):
                        src = PSY[t][:, :].rearrange(
                            "p (a r) -> p a r", a=2)[:, :, 0:260].rearrange(
                            "p a (b c) -> p a b c", c=130)
                        base = YACC_F + ((t * 4) * 8 + pair) * 130
                        dstsl = ARF[:, base:base + 4 * 1040].rearrange(
                            "p (a b r) -> p a b r", a=2, r=1040)[:, :, :, 0:130]
                        if pos == 0:
                            def ev(en, dstsl=dstsl, src=src):
                                return en.tensor_copy(dstsl, src)
                        else:
                            def ev(en, dstsl=dstsl, src=src):
                                return en.tensor_tensor(dstsl, dstsl, src,
                                                        ALU.add)
                        vdvY = dvv(ev)
                    drain("vector")
                    st2["psY_evac_dv"] = vdvY
                emit_wq_pump(8, XQR_C)

        wp_ld = {}

        def slot_post(slot):
            yn_st = 0
            for qsub in range(8):
                base = YACC_F + qsub * 8 * 130
                yv = ARF[:, base:base + 8 * 130]
                dens = yv.rearrange("p (r c) -> p r c", c=130)[:, :, 64:130:65]
                rec = ff(RECN_F, 16).rearrange("p (r c) -> p r c", c=2)
                W("vector", "dv", st2["psY_evac_dv"])
                if qsub == 0:
                    drain("vector")
                dvv(lambda en, dens=dens, rec=rec: en.reciprocal(rec, dens),
                    inc=False)
                drain("vector")
                ys = yv.rearrange("p (r hc) -> p r hc", hc=130).rearrange(
                    "p r (h c) -> p r h c", c=65)[:, :, :, 0:64]
                yn = bf(YN_STG_C, 1024).rearrange(
                    "p (r h c) -> p r h c", r=8, c=64)
                if qsub > 0:
                    W("vector", "dC", yn_st)
                vdvN = dvv(lambda en, ys=ys, yn=yn, rec=rec: en.tensor_tensor(
                    yn, ys,
                    rec.unsqueeze(-1).broadcast_to([P, 8, 2, 64]), ALU.mult))
                drain("vector")
                W("scalar", "dv", vdvN)
                yn_st = dma("scalar", "dC",
                            ap["ynorm_dram"][slot, qsub * P:(qsub + 1) * P, :],
                            bf(YN_STG_C, 1024))
                if debug:
                    dma("scalar", "dC",
                        ap["dbg_yn"][slot, qsub * P:(qsub + 1) * P, :],
                        bf(YN_STG_C, 1024))
            W("sync", "dC", yn_st)
            W("sync", "pe", st2["kt_last_pe"][0])
            ynld = 0
            for cc in range(8):
                ynld = dma("sync", "dS", bf(YNT_C + cc * 1024, 1024),
                           ap["ynorm_dram"][slot][:, cc * P:(cc + 1) * P],
                           transpose=True)
            ynt3 = bf(YNT_C, 8192).rearrange("p (a t) -> p a t", a=8)
            W("tensor", "dS", ynld)
            W("tensor", "dv", st2["psY_evac_dv"])
            vpeU = None
            for half in range(2):
                for a in range(8):
                    def mmu(en, a=a, half=half, ynt3=ynt3):
                        return en.matmul(
                            PS_Y0[0:32, half * 512:(half + 1) * 512],
                            bf(LAPT_C + a * 32, 32),
                            ynt3[:, a, half * 512:(half + 1) * 512],
                            start=(a == 0), stop=(a == 7))
                    if half == 1 and a == 7:
                        vpeU = pe(mmu)
                    else:
                        pe(mmu, inc=False)
            W("scalar", "pe", vpeU)
            W("scalar", "pe", st2["proj_last_pe"])
            acc(lambda en: en.activation(u2b[0:32, 0:512],
                                         PS_Y0[0:32, 0:512], AF.Copy,
                                         scale=0.5), inc=False)
            vacU2 = acc(lambda en: en.activation(u2b[0:32, 512:1024],
                                                 PS_Y0[0:32, 512:1024],
                                                 AF.Copy, scale=0.5))
            st2["psY0_ac"] = vacU2
            W("gpsimd", "pe", st2["kt_last_pe"][1])
            for a in range(8):
                wp_ld[(slot, a)] = dma(
                    "gpsimd", "dG", bf(KT1_C + a * 1024, 1024),
                    ap["wpT"][a * P:(a + 1) * P, :])
            W("tensor", "ac", vacU2)
            W("tensor", "ac", st2["s_free"][0])
            prime()
            for tsub in range(8):
                sa = tsub % 2
                ps = PS_SA if sa == 0 else PS_SB
                W("tensor", "ac", st2["s_free"][sa])
                vpeP = None
                for a in range(8):
                    if tsub == 0:
                        W("tensor", "dG", wp_ld[(slot, a)])
                    for half in range(2):
                        def mmp(en, a=a, half=half, ps=ps, tsub=tsub):
                            return en.matmul(
                                ps[:, half * 512:(half + 1) * 512],
                                ynt3[:, a, tsub * 128:(tsub + 1) * 128],
                                bf(KT1_C + a * 1024, 1024)
                                [:, half * 512:(half + 1) * 512],
                                start=(a == 0), stop=False)
                        pe(mmp, inc=False)
                for half in range(2):
                    def mmpl(en, half=half, ps=ps, tsub=tsub):
                        return en.matmul(
                            ps[:, half * 512:(half + 1) * 512],
                            u2b[:, tsub * 128:(tsub + 1) * 128],
                            lbp[:, half * 512:(half + 1) * 512],
                            start=False, stop=True)
                    if half == 1:
                        vpeP = pe(mmpl)
                    else:
                        pe(mmpl, inc=False)
                st2["proj_last_pe"] = vpeP
                st2["kt_last_pe"][0] = vpeP
                st2["kt_last_pe"][1] = vpeP
                W("scalar", "pe", vpeP)
                W("scalar", "dv", st2.get("pstg_free_dv", 0))
                W("scalar", "dC", st2.get("pstg_free_dc", 0))
                acc(lambda en, ps=ps: en.activation(
                    ff(PSTG_F, 512), ps[:, 0:512], AF.Copy), inc=False)
                vacP = acc(lambda en, ps=ps: en.activation(
                    ff(PSTG_F + 512, 512), ps[:, 512:1024], AF.Copy))
                st2["s_free"][sa] = vacP
                row0 = slot * 1024 + tsub * P
                W("sync", "dv", st2.get("pstg2_free_dv", 0))
                vxr = dma("sync", "dS", ff(PSTG2_F, 1024),
                          ap["xres"][row0:row0 + P, :])
                W("vector", "ac", vacP)
                if debug:
                    vdbg = dma("scalar", "dC", ap["dbg_p"][row0:row0 + P, :],
                               ff(PSTG_F, 1024))
                    W("vector", "dC", vdbg)
                W("vector", "dS", vxr)
                vres = dvv(lambda en: en.tensor_tensor(
                    ff(PSTG_F, 1024), ff(PSTG_F, 1024), ff(PSTG2_F, 1024),
                    ALU.add))
                st2["pstg2_free_dv"] = vres
                drain("vector")
                dvv(lambda en: en.bn_stats(ff(BN_F, 6), ff(PSTG_F, 512)),
                    inc=False)
                dvv(lambda en: en.bn_stats(ff(BN_F + 6, 6),
                                           ff(PSTG_F + 512, 512)), inc=False)
                drain("vector")
                vdvS = dvv(lambda en: en.bn_aggr(ff(BN_F + 12, 2),
                                                 ff(BN_F, 12)))
                drain("vector")
                dvv(lambda en: en.tensor_scalar(
                    ff(BN_F + 13, 1), ff(BN_F + 13, 1), LNEPS, None,
                    ALU.add), inc=False)
                drain("vector")
                vdvS = dvv(lambda en: en.tensor_scalar(
                    ff(BN_F + 15, 1), ff(BN_F + 13, 1), 1.0, None, ALU.mult))
                W("scalar", "dv", vdvS)
                acc(lambda en: en.activation(
                    ff(BN_F + 14, 1), ff(BN_F + 13, 1), AF.Ln), inc=False)
                drain("scalar")
                vacR = acc(lambda en: en.activation(
                    ff(BN_F + 14, 1), ff(BN_F + 14, 1), AF.Exp, scale=-0.5))
                W("vector", "ac", vacR)
                vdvX = dvv(lambda en: en.tensor_scalar(
                    ff(PSTG_F, 1024), ff(PSTG_F, 1024), ff(BN_F + 12, 1),
                    ff(BN_F + 14, 1), ALU.subtract, ALU.mult))
                drain("vector")
                W("scalar", "dv", vdvX)
                vx1 = dma("scalar", "dC", ap["x1_dram"][row0:row0 + P, :],
                          ff(PSTG_F, 1024))
                if debug:
                    dma("scalar", "dC", ap["dbg_x1"][row0:row0 + P, :],
                        ff(PSTG_F, 1024))
                W("vector", "dC", wq_state["obf_free"])
                W("vector", "dC", st2["x1q_free"])
                vq = emit_quant(ff(PSTG_F, 1024), bf(XQR_C, 1024), 1024,
                                ff(QSA_F, 1024), ff(QSB_F, 1024),
                                ff(QSC_F, 1024),
                                ff(SM_F, 64), ff(SM_F + 64, 64),
                                ff(SM_F + 128, 64), ff(SM_F + 192, 64))
                wq_state["stg_free"] = vq
                st2["pstg_free_dv"] = vq
                st2["pstg_free_dc"] = vx1
                W("scalar", "dv", vq)
                vxq = dma("scalar", "dC", ap["xq_dram"][row0:row0 + P, :],
                          bf(XQR_C, 1024))
                if debug:
                    dma("scalar", "dC", ap["dbg_xq"][row0:row0 + P, :],
                        bf(XQR_C, 1024))
                wq_state["obf_free"] = vxq
                st2["x1q_free"] = vxq

        attention_slot(0)
        slot_post(0)
        attention_slot(1)
        slot_post(1)
        while wq_state["i"] < len(wq_tiles) or wq_pending:
            emit_wq_pump(16, XQR_C)

        for r in range(C // P):
            stg = ff(H0_F, I_FFN)
            obf = bf(WDQ3_C, I_FFN)
            W("sync", "dv", wq_state["stg_free"])
            ld = dma("sync", "dS", stg, ap["wd"][r * P:(r + 1) * P, :])
            W("vector", "dS", ld)
            W("vector", "dC", wq_state["obf_free"])
            vq = emit_quant(stg, obf, I_FFN,
                            ff(Q31_F, I_FFN), ff(Q32_F, I_FFN),
                            ff(Q33_F, I_FFN),
                            ff(SM_F, 176), ff(SM_F + 176, 176),
                            ff(R3A_F, 176), ff(R3B_F, 176))
            wq_state["stg_free"] = vq
            W("scalar", "dv", vq)
            wq_state["obf_free"] = dma("scalar", "dC",
                                       ap["wdq_dram"][r * P:(r + 1) * P, :],
                                       obf)

        W("sync", "dC", sems["dC"].v)
        tstg = bf(WBG0_C, I_FFN)
        vtr = 0
        for src_d, dst_d, n, wdt2 in (("wgq_dram", "wgqT_dram", 8, I_FFN),
                                      ("wuq_dram", "wuqT_dram", 8, I_FFN),
                                      ("wdq_dram", "wdqT_dram", NIC, C)):
            for ccx in range(n):
                W("sync", "dC", vtr if False else 0)
                vl = dma("sync", "dS", tstg[:, 0:wdt2],
                         ap[src_d][:, ccx * P:(ccx + 1) * P], transpose=True)
                W("scalar", "dS", vl)
                vtr = dma("scalar", "dC", ap[dst_d][ccx], tstg[:, 0:wdt2])
                W("sync", "dC", vtr)
        W("sync", "dC", sems["dC"].v)
        v_xqt = 0
        for cc in range(8):
            v_xqt = dma("sync", "dS", bf(XQT_C + cc * 2048, 2048),
                        ap["xq_dram"][:, cc * P:(cc + 1) * P], transpose=True)
        xqt3 = bf(XQT_C, 16384).rearrange("p (a t) -> p a t", a=8)

        IBANDS = [(0, 512), (512, 512), (1024, 512), (1536, 512),
                  (2048, 512), (2560, 256)]
        wb_g = [bf(WBG0_C, 4096), bf(WBG1_C, 4096)]
        wb_u = [bf(WBU0_C, 4096), bf(WBU1_C, 4096)]
        st3 = {"wb_free_pe": [0, 0], "hq_free_dc": 0,
               "gu_free_ac": [0, 0], "gu_free_dv": [0, 0],
               "h_free": [0, 0]}

        def load_band2(ib, wb):
            i0, wdt = IBANDS[ib]
            W("sync", "pe", st3["wb_free_pe"][wb])
            dma("sync", "dS",
                wb_g[wb].rearrange("p (a w) -> p a w", a=8)[:, :, 0:wdt],
                ap["wgqT_dram"][:, :, i0:i0 + wdt].rearrange(
                    "a p w -> p a w"))
            return dma("sync", "dS",
                       wb_u[wb].rearrange("p (a w) -> p a w", a=8)
                       [:, :, 0:wdt],
                       ap["wuqT_dram"][:, :, i0:i0 + wdt].rearrange(
                           "a p w -> p a w"))

        hbufs = (ff(H0_F, I_FFN), ff(H1_F, I_FFN))
        PSG = (PS_SA, PS_SB)
        PSU = (PS_Y0, PS_Y1)
        band_ld = {(0, 0): load_band2(0, 0)}
        W("tensor", "ac", st2["s_free"][0])
        prime()

        for grp in range(8):
            for ib in range(6):
                wb = ib % 2
                i0, wdt = IBANDS[ib]
                ldv = band_ld.pop((grp, ib))
                if ib + 1 < 6:
                    band_ld[(grp, ib + 1)] = load_band2(ib + 1, 1 - wb)
                elif grp + 1 < 8:
                    band_ld[(grp + 1, 0)] = load_band2(0, 1 - wb)
                W("tensor", "dS", ldv)
                if grp == 0 and ib == 0:
                    W("tensor", "dS", v_xqt)
                for ts in range(2):
                    tsub = grp * 2 + ts
                    psg, psu = PSG[ts], PSU[ts]
                    W("tensor", "ac", st3["gu_free_ac"][ts])
                    W("tensor", "dv", st3["gu_free_dv"][ts])
                    vpeGU = None
                    for a in range(8):
                        def mmg(en, a=a, tsub=tsub, wb=wb, wdt=wdt, psg=psg):
                            return en.matmul(
                                psg[:, 0:wdt],
                                xqt3[:, a, tsub * 128:(tsub + 1) * 128],
                                wb_g[wb][:, a * 512:a * 512 + wdt],
                                start=(a == 0), stop=(a == 7))
                        pe(mmg, inc=False)
                        def mmu(en, a=a, tsub=tsub, wb=wb, wdt=wdt, psu=psu):
                            return en.matmul(
                                psu[:, 0:wdt],
                                xqt3[:, a, tsub * 128:(tsub + 1) * 128],
                                wb_u[wb][:, a * 512:a * 512 + wdt],
                                start=(a == 0), stop=(a == 7))
                        if a == 7:
                            vpeGU = pe(mmu)
                        else:
                            pe(mmu, inc=False)
                    st3["wb_free_pe"][wb] = vpeGU
                    W("scalar", "pe", vpeGU)
                    W("scalar", "dv", st3.get("sil_free_dv", 0))
                    vsil = acc(lambda en, psg=psg, wdt=wdt: en.activation(
                        ff(SIL_F, wdt), psg[:, 0:wdt], AF.Silu))
                    st3["gu_free_ac"][ts] = vsil
                    W("vector", "ac", vsil)
                    W("vector", "dv", st3["h_free"][ts])
                    vh = dvv(lambda en, ts=ts, i0=i0, wdt=wdt, psu=psu:
                             en.tensor_tensor(
                                 hbufs[ts][:, i0:i0 + wdt],
                                 ff(SIL_F, wdt), psu[:, 0:wdt], ALU.mult))
                    st3["gu_free_dv"][ts] = vh
                    st3["sil_free_dv"] = vh
                    drain("vector")
            for ts in range(2):
                tsub = grp * 2 + ts
                W("vector", "dC", st3["hq_free_dc"])
                vq = emit_quant(hbufs[ts], bf(HQS_C, I_FFN), I_FFN,
                                ff(Q31_F, I_FFN), ff(Q32_F, I_FFN),
                                ff(Q33_F, I_FFN),
                                ff(SM_F, 176), ff(SM_F + 176, 176),
                                ff(R3A_F, 176), ff(R3B_F, 176))
                st3["h_free"][ts] = vq
                W("scalar", "dv", vq)
                st3["hq_free_dc"] = dma(
                    "scalar", "dC",
                    ap["hq_dram"][tsub * P:(tsub + 1) * P, :],
                    bf(HQS_C, I_FFN))

        W("sync", "dC", sems["dC"].v)
        W("sync", "dv", sems["dv"].v)
        W("sync", "pe", st3["wb_free_pe"][0])
        W("sync", "pe", st3["wb_free_pe"][1])
        vtr = 0
        for icx in range(NIC):
            vl = dma("sync", "dS", bf(WBG0_C, 2048),
                     ap["hq_dram"][:, icx * P:(icx + 1) * P], transpose=True)
            W("scalar", "dS", vl)
            vtr = dma("scalar", "dC", ap["hqT_dram"][icx], bf(WBG0_C, 2048))
            W("sync", "dC", vtr)
        hqt = [bf(HQT0_C, 2048), bf(HQT1_C, 2048)]
        wdt_ = [bf(WDT0_C, 1024), bf(WDT1_C, 1024)]
        st4 = {"ic_free_pe": [0, 0], "ps_free": [0, 0, 0, 0]}
        PSD = (PS_SA, PS_SB, PS_Y0, PS_Y1)

        def load_ic(ic, buf):
            W("sync", "pe", st4["ic_free_pe"][buf])
            dma("sync", "dS", hqt[buf], ap["hqT_dram"][ic])
            return dma("sync", "dS", wdt_[buf], ap["wdqT_dram"][ic])

        ld_next = None
        W("tensor", "ac", st3["gu_free_ac"][0])
        prime()
        for tg in range(4):
            ld_ic = {0: load_ic(0, 0)} if ld_next is None else ld_next
            ld_next = None
            vpeD = None
            for ic in range(NIC):
                buf = ic % 2
                if ic + 1 < NIC:
                    ld_ic[ic + 1] = load_ic(ic + 1, 1 - buf)
                elif tg + 1 < 4:
                    ld_next = {0: load_ic(0, 1 - buf)}
                W("tensor", "dS", ld_ic[ic])
                for ts in range(4):
                    tsub = tg * 4 + ts
                    ps = PSD[ts]
                    if ic == 0:
                        W("tensor", "ac", st4["ps_free"][ts])
                    for half in range(2):
                        def mmd(en, half=half, ic=ic, tsub=tsub, buf=buf,
                                ps=ps):
                            return en.matmul(
                                ps[:, half * 512:(half + 1) * 512],
                                hqt[buf][:, tsub * 128:tsub * 128 + 128],
                                wdt_[buf][:, half * 512:(half + 1) * 512],
                                start=(ic == 0), stop=(ic == NIC - 1))
                        if half == 1 and ts == 3:
                            vpeD = pe(mmd)
                        else:
                            pe(mmd, inc=False)
                st4["ic_free_pe"][buf] = vpeD
            for ts in range(4):
                tsub = tg * 4 + ts
                ps = PSD[ts]
                row0 = tsub * P
                W("scalar", "pe", vpeD)
                W("scalar", "dC", st4.get("pstg3_free_dc", 0))
                acc(lambda en, ps=ps: en.activation(
                    ff(PSTG3_F, 512), ps[:, 0:512], AF.Copy), inc=False)
                vacD = acc(lambda en, ps=ps: en.activation(
                    ff(PSTG3_F + 512, 512), ps[:, 512:1024], AF.Copy))
                st4["ps_free"][ts] = vacD
                W("sync", "dv", st4.get("x1s_free_dv", 0))
                vx1 = dma("sync", "dS", ff(X1S_F, 1024),
                          ap["x1_dram"][row0:row0 + P, :])
                W("vector", "ac", vacD)
                W("vector", "dS", vx1)
                vres2 = dvv(lambda en: en.tensor_tensor(
                    ff(PSTG3_F, 1024), ff(PSTG3_F, 1024), ff(X1S_F, 1024),
                    ALU.add))
                st4["x1s_free_dv"] = vres2
                drain("vector")
                dvv(lambda en: en.bn_stats(ff(BN3_F, 6), ff(PSTG3_F, 512)),
                    inc=False)
                dvv(lambda en: en.bn_stats(ff(BN3_F + 6, 6),
                                           ff(PSTG3_F + 512, 512)),
                    inc=False)
                drain("vector")
                vdvS = dvv(lambda en: en.bn_aggr(ff(BN3_F + 12, 2),
                                                 ff(BN3_F, 12)))
                drain("vector")
                dvv(lambda en: en.tensor_scalar(
                    ff(BN3_F + 13, 1), ff(BN3_F + 13, 1), LNEPS, None,
                    ALU.add), inc=False)
                drain("vector")
                vdvS = dvv(lambda en: en.tensor_scalar(
                    ff(BN3_F + 15, 1), ff(BN3_F + 13, 1), 1.0, None,
                    ALU.mult))
                W("scalar", "dv", vdvS)
                acc(lambda en: en.activation(
                    ff(BN3_F + 14, 1), ff(BN3_F + 13, 1), AF.Ln), inc=False)
                drain("scalar")
                vacR = acc(lambda en: en.activation(
                    ff(BN3_F + 14, 1), ff(BN3_F + 14, 1), AF.Exp,
                    scale=-0.5))
                W("vector", "ac", vacR)
                vdvX = dvv(lambda en: en.tensor_scalar(
                    ff(PSTG3_F, 1024), ff(PSTG3_F, 1024), ff(BN3_F + 12, 1),
                    ff(BN3_F + 14, 1), ALU.subtract, ALU.mult))
                drain("vector")
                W("scalar", "dv", vdvX)
                st4["pstg3_free_dc"] = dma(
                    "scalar", "dC", ap["out"][row0:row0 + P, :],
                    ff(PSTG3_F, 1024))

        W("sync", "dC", sems["dC"].v)
        W("gpsimd", "dC", sems["dC"].v)

        @block.sync
        def _(en):
            for f in ops["sync"]:
                f(en)

        @block.gpsimd
        def _(en):
            for f in ops["gpsimd"]:
                f(en)

        @block.tensor
        def _(en):
            for f in ops["tensor"]:
                f(en)

        @block.vector
        def _(en):
            for f in ops["vector"]:
                f(en)

        @block.scalar
        def _(en):
            for f in ops["scalar"]:
                f(en)

    return nc, ap


def kernel(**inputs):
    in_maps = _host_prepare(inputs)
    nc, _ = build_program(debug=False)
    res = run_bass_kernel_spmd(nc, in_maps, core_ids=list(range(8)))
    outp = np.empty((2, T, C), np.float32)
    for core in range(8):
        b, g = core // 4, core % 4
        o = np.asarray(res.results[core]["out"])
        outp[b, g * SUP:(g + 1) * SUP] = o[0:SUP]
        outp[b, (7 - g) * SUP:(8 - g) * SUP] = o[SUP:2 * SUP]
    return outp


# version 12
# speedup vs baseline: 1.1264x; 1.0011x over previous
import numpy as np
import ml_dtypes
import concourse.bass as bass
import concourse.mybir as mybir
from concourse.bass_utils import run_bass_kernel_spmd

F32 = mybir.dt.float32
BF16 = mybir.dt.bfloat16
I32 = mybir.dt.int32
AF = mybir.ActivationFunctionType
ALU = mybir.AluOpType
AX = mybir.AxisListType

P = 128
T = 8192
C = 1024
NS = 8
SUP = 1024
NPAIR = 8
I_FFN = 2816
COVER = (4, 8)
DIAGPOS = (3, 7)
NTOK = 2048
LNEPS = 1e-5
NIC = I_FFN // P

BF2 = np.dtype(ml_dtypes.bfloat16)

QT_C = 0
LBQ_C = 16384
LBK_C = 17408
LBV_C = 18432
LAT_C = 19456
UBP1_C = 19712
MSK_C = 16384
WKR_C = 24576
WVR_C = 32768
KT0_C = 24576
KT1_C = 32768
YNT_C = 24576
XTB0_C = 40960
XTB1_C = 49280
V0_C = 40960
V1_C = 49280
E_C = 57600
SQB_C = 62720
KSTG0_C = 63744
KSTG1_C = 64768
VSTG0_C = 65792
VSTG1_C = 66832
XQR_C = 65792
ONE_C = 67872
LAPT_C = 67873
LBP_C = 68129
YN_STG_C = 69153
ARB_N = 70400
XQT_C = 0
WBG0_C = 16384
WBG1_C = 20480
WBU0_C = 24576
WBU1_C = 28672
HQS_C = 32768
WDQ3_C = 35584
HQT0_C = 38400
HQT1_C = 40448
WDT0_C = 42496
WDT1_C = 43520

YACC_F = 0
SENT_F = 8320
CNEG_F = 8336
BIAS_F = 8344
KMX_F = 8472
QMX_F = 8536
KM2_F = 8552
QM2_F = 8560
CSC_F = 8568
BN_F = 8576
PSTG_F = 8600
PSTG2_F = 9624
QSA_F = 10648
QSB_F = 11672
QSC_F = 12696
WQS_F = 13720
DUMP_F = 14744
SM_F = 15256
BN2_F = 15608
U2B_F = 16384
PSTGB_F = 16896
ARF_N = 17920
RECN_F = SM_F + 280
H0_F = 0
H1_F = 2816
Q31_F = 5632
Q32_F = 8448
Q33_F = 11264
X1S_F = 14080
R3A_F = DUMP_F
R3B_F = DUMP_F + 176
BN3_F = SM_F + 400
PSTG3_F = QSA_F
SIL_F = 15800


def _perm_for_core(g):
    d0, d1 = g, 7 - g
    others = [s for s in range(8) if s not in (d0, d1)]
    first = [s for s in others if s < d0]
    rest = [s for s in others if s >= d0]
    p012 = (first + rest)[:3]
    remaining = [s for s in others if s not in p012]
    perm = p012 + [d0] + remaining + [d1]
    assert len(perm) == 8 and sorted(perm) == list(range(8))
    return perm


def _host_prepare(inp):
    x = np.asarray(inp["L0_attn_x"], np.float32)
    w_attn = np.asarray(inp["L0_attn_w_attn"], np.float32)
    b_attn = np.asarray(inp["L0_attn_b_attn"], np.float32)
    la = np.asarray(inp["L0_attn_la_attn"], np.float32)
    lb = np.asarray(inp["L0_attn_lb_attn"], np.float32)
    w_proj = np.asarray(inp["L0_attn_w_proj"], np.float32)
    b_proj = np.asarray(inp["L0_attn_b_proj"], np.float32)
    lap = np.asarray(inp["L0_attn_la_proj"], np.float32)
    lbp = np.asarray(inp["L0_attn_lb_proj"], np.float32)

    shared = {
        "wqT": np.ascontiguousarray(w_attn[0:1024].T),
        "wkT": np.ascontiguousarray(w_attn[1024:2048].T),
        "wvT": np.ascontiguousarray(w_attn[2048:3072].T),
        "laT": np.ascontiguousarray(la.T),
        "lbqA": np.ascontiguousarray(
            np.concatenate([lb[0:1024].T, b_attn[None, 0:1024]], 0)),
        "lbkA": np.ascontiguousarray(
            np.concatenate([lb[1024:2048].T, b_attn[None, 1024:2048]], 0)),
        "lbvA": np.ascontiguousarray(
            np.concatenate([lb[2048:3072].T, b_attn[None, 2048:3072]], 0)),
        "wpT": np.ascontiguousarray(w_proj.T),
        "lapT": np.ascontiguousarray(lap.T),
        "lbpA": np.ascontiguousarray(
            np.concatenate([lbp.T, b_proj[None, :]], 0)),
        "wg": np.asarray(inp["L0_ffn_w_gate"], np.float32),
        "wu": np.asarray(inp["L0_ffn_w_up"], np.float32),
        "wd": np.asarray(inp["L0_ffn_w_down"], np.float32),
    }
    kk = np.arange(P)[:, None]
    qq = np.arange(SUP)[None, :]
    m = np.empty((P, NS * SUP), np.float32)
    for j in range(8):
        m[:, j * SUP:(j + 1) * SUP] = ((j * P + kk) <= qq)
    shared["dmask"] = m.astype(BF2)

    in_maps = []
    for core in range(8):
        b, g = core // 4, core % 4
        perm = _perm_for_core(g)
        xT = x[b].T
        d = dict(shared)
        d["xTperm"] = np.ascontiguousarray(
            xT.reshape(C, NS, SUP)[:, perm, :].reshape(C, T))
        own = [g, 7 - g]
        d["xTown"] = np.ascontiguousarray(
            xT.reshape(C, NS, SUP)[:, own, :].reshape(C, NTOK))
        d["xres"] = np.ascontiguousarray(
            x[b].reshape(NS, SUP, C)[own].reshape(NTOK, C))
        sent = np.full((P, 16), -1.0e30, np.float32)
        for slot in range(2):
            qs = own[slot]
            for pos in range(COVER[slot]):
                s_at = perm[pos]
                valid = (s_at < qs) or (pos == DIAGPOS[slot])
                sent[:, slot * 8 + pos] = 1.0e30 if valid else -1.0e30
        d["sent"] = sent
        in_maps.append(d)
    return in_maps


class _Sem:
    def __init__(self, handle):
        self.h = handle
        self.v = 0


def build_program(debug=False):
    nc = bass.Bass("TRN2", target_bir_lowering=False, debug=False)
    ap = {}

    def din(name, shape, dt):
        ap[name] = nc.dram_tensor(name, shape, dt, kind="ExternalInput").ap()

    def dint(name, shape, dt):
        ap[name] = nc.dram_tensor(name, shape, dt, kind="Internal").ap()

    din("xTperm", [C, T], F32)
    din("xTown", [C, NTOK], F32)
    din("xres", [NTOK, C], F32)
    din("wqT", [C, C], F32)
    din("wkT", [C, C], F32)
    din("wvT", [C, C], F32)
    din("laT", [C, 32], F32)
    din("lbqA", [33, C], F32)
    din("lbkA", [33, C], F32)
    din("lbvA", [33, C], F32)
    din("wpT", [C, C], F32)
    din("lapT", [C, 32], F32)
    din("lbpA", [33, C], F32)
    din("wg", [I_FFN, C], F32)
    din("wu", [I_FFN, C], F32)
    din("wd", [C, I_FFN], F32)
    din("dmask", [P, NS * SUP], BF16)
    din("sent", [P, 16], F32)

    dint("kt_dram", [NS, NPAIR, P, SUP], BF16)
    dint("v_dram", [NS, 8, P, 16 * 65], BF16)
    dint("cneg_dram", [1, NPAIR], F32)
    dint("x1_dram", [NTOK, C], F32)
    dint("xq_dram", [NTOK, C], BF16)
    dint("ynorm_dram", [2, SUP, C], BF16)
    dint("hq_dram", [NTOK, I_FFN], BF16)
    dint("wgq_dram", [I_FFN, C], BF16)
    dint("wuq_dram", [I_FFN, C], BF16)
    dint("wdq_dram", [C, I_FFN], BF16)
    dint("wgqT_dram", [8, P, I_FFN], BF16)
    dint("wuqT_dram", [8, P, I_FFN], BF16)
    dint("wdqT_dram", [NIC, P, C], BF16)
    dint("hqT_dram", [NIC, P, NTOK], BF16)

    ap["out"] = nc.dram_tensor("out", [NTOK, C], F32,
                               kind="ExternalOutput").ap()
    if debug:
        for nm, shp, dt in [("dbg_p", [NTOK, C], F32),
                            ("dbg_x1", [NTOK, C], F32),
                            ("dbg_xq", [NTOK, C], BF16),
                            ("dbg_cneg", [1, NPAIR], F32),
                            ("dbg_yn", [2, SUP, C], BF16)]:
            ap[nm] = nc.dram_tensor(nm, shp, dt, kind="ExternalOutput").ap()

    from contextlib import ExitStack
    with ExitStack() as ctx:
        e = ctx.enter_context
        ARB = e(nc.sbuf_tensor("ARB", [P, ARB_N], BF16))
        ARF = e(nc.sbuf_tensor("ARF", [P, ARF_N], F32))

        def bf(c0, n, p0=0, prows=P):
            return ARB[p0:p0 + prows, c0:c0 + n]

        def ff(c0, n, p0=0, prows=P):
            return ARF[p0:p0 + prows, c0:c0 + n]

        PS_SA = e(nc.psum_tensor("PS_SA", [P, 1024], F32))
        PS_SB = e(nc.psum_tensor("PS_SB", [P, 1024], F32))
        PS_Y0 = e(nc.psum_tensor("PS_Y0", [P, 1024], F32))
        PS_Y1 = e(nc.psum_tensor("PS_Y1", [P, 1024], F32))
        PSY = (PS_Y0, PS_Y1)

        sems = {}
        for name in ("dS", "dG", "dC", "pe", "ac", "dv", "gp"):
            sems[name] = _Sem(e(nc.semaphore(name)))
        block = e(nc.Block())

        ops = {k: [] for k in ("sync", "gpsimd", "tensor", "vector", "scalar")}

        def W(eng, sem, val):
            if val is None or val <= 0:
                return
            h = sems[sem].h
            ops[eng].append(lambda en: en.wait_ge(h, val))

        def dma(eng, sem, outap, inap, transpose=False):
            s = sems[sem]
            s.v += 16
            h = s.h
            if transpose:
                ops[eng].append(
                    lambda en: en.dma_start_transpose(outap, inap)
                    .then_inc(h, 16))
            else:
                ops[eng].append(
                    lambda en: en.dma_start(out=outap, in_=inap)
                    .then_inc(h, 16))
            return s.v

        def _mk(eng, sem):
            def run(fn, inc=True):
                if inc:
                    s = sems[sem]
                    s.v += 1
                    h = s.h
                    ops[eng].append(lambda en: fn(en).then_inc(h, 1))
                    return s.v
                ops[eng].append(fn)
                return None
            return run

        pe = _mk("tensor", "pe")
        acc = _mk("scalar", "ac")
        dvv = _mk("vector", "dv")
        gpc = _mk("gpsimd", "gp")

        def drain(eng):
            ops[eng].append(lambda en: en.drain())

        v_sent = dma("sync", "dS", ff(SENT_F, 16), ap["sent"][:, :])
        dma("gpsimd", "dG", bf(LAPT_C, 256).rearrange("p (a n) -> p a n", a=8),
            ap["lapT"].rearrange("(a p) r -> p a r", p=P))
        dma("gpsimd", "dG", bf(LBP_C, 1024, 0, 33), ap["lbpA"][:, :])
        dma("gpsimd", "dG", bf(LBQ_C, 1024, 0, 33), ap["lbqA"][:, :])
        dma("gpsimd", "dG", bf(LBK_C, 1024, 0, 33), ap["lbkA"][:, :])
        dma("gpsimd", "dG", bf(LBV_C, 1024, 0, 33), ap["lbvA"][:, :])
        dma("gpsimd", "dG", bf(LAT_C, 256).rearrange("p (a n) -> p a n", a=8),
            ap["laT"].rearrange("(a p) r -> p a r", p=P))
        gpc(lambda en: en.memset(bf(ONE_C, 1), 1.0), inc=False)
        gpc(lambda en: en.memset(bf(UBP1_C, 1024, 32, 1), 1.0), inc=False)
        gpc(lambda en: en.memset(
            ARF[32:33, U2B_F:U2B_F + 512].bitcast(BF16), 1.0), inc=False)
        gpc(lambda en: en.memset(bf(VSTG0_C, 1040).rearrange(
            "p (h w) -> p h w", w=65)[:, :, 64:65], 1.0), inc=False)
        v_ms = gpc(lambda en: en.memset(bf(VSTG1_C, 1040).rearrange(
            "p (h w) -> p h w", w=65)[:, :, 64:65], 1.0))
        v_wk = dma("gpsimd", "dG", bf(WKR_C, 8192).rearrange("p (a n) -> p a n", a=8),
                   ap["wkT"].rearrange("(a p) o -> p a o", p=P))
        v_wv = dma("gpsimd", "dG", bf(WVR_C, 8192).rearrange("p (a n) -> p a n", a=8),
                   ap["wvT"].rearrange("(a p) o -> p a o", p=P))

        xtb = [bf(XTB0_C, 8192), bf(XTB1_C, 8192)]
        kstg = [bf(KSTG0_C, 1024), bf(KSTG1_C, 1024)]
        vstg = [bf(VSTG0_C, 1040), bf(VSTG1_C, 1040)]
        sqb_b = [bf(E_C, 1024), bf(E_C + 1024, 1024)]
        ub = bf(UBP1_C, 1024, 0, 33)
        u2b = ARF[0:33, U2B_F:U2B_F + 512].bitcast(BF16)
        wkr = bf(WKR_C, 8192).rearrange("p (a o) -> p a o", a=8)
        wvr = bf(WVR_C, 8192).rearrange("p (a o) -> p a o", a=8)
        laTb = bf(LAT_C, 256).rearrange("p (a r) -> p a r", a=8)
        lbq = bf(LBQ_C, 1024, 0, 33)
        lbk = bf(LBK_C, 1024, 0, 33)
        lbv = bf(LBV_C, 1024, 0, 33)
        lbp = bf(LBP_C, 1024, 0, 33)
        onec = bf(ONE_C, 1)

        def xsrc(pos):
            return ap["xTperm"][:, pos * SUP:(pos + 1) * SUP].rearrange(
                "(a p) t -> p a t", p=P)

        wq_tiles = []
        for wname, dst in (("wg", "wgq_dram"), ("wu", "wuq_dram")):
            for r in range(NIC):
                wq_tiles.append((ap[wname][r * P:(r + 1) * P, :],
                                 ap[dst][r * P:(r + 1) * P, :]))
        wq_state = {"i": 0, "stg_free": 0, "obf_free": 0, "ld": 0}

        def emit_quant(src, dst, width, s1, s2, s3, mblk, sblk, rblk, s2blk):
            nb = width // 16
            x3 = src.rearrange("p (b k) -> p b k", k=16)
            dvv(lambda en: en.tensor_reduce(mblk, x3, AX.X, ALU.max,
                                            apply_absolute_value=True),
                inc=False)
            drain("vector")
            dvv(lambda en: en.tensor_scalar(sblk, mblk, 1.0 / 6.0, 1e-12,
                                            ALU.mult, ALU.max), inc=False)
            drain("vector")
            dvv(lambda en: en.reciprocal(rblk, sblk), inc=False)
            dvv(lambda en: en.tensor_scalar(s2blk, sblk, 2.0, None, ALU.mult),
                inc=False)
            drain("vector")
            dvv(lambda en: en.tensor_tensor(
                s1.rearrange("p (b k) -> p b k", k=16), x3,
                rblk.unsqueeze(-1).broadcast_to([P, nb, 16]), ALU.mult),
                inc=False)
            drain("vector")
            dvv(lambda en: en.tensor_scalar(
                s2.bitcast(I32), s1.bitcast(I32), 0x7FFFFFFF, None,
                ALU.bitwise_and), inc=False)
            drain("vector")
            dvv(lambda en: en.tensor_scalar(s3, s2, 1.0, None, ALU.max),
                inc=False)
            drain("vector")
            dvv(lambda en: en.tensor_scalar(
                s3.bitcast(I32), s3.bitcast(I32), 8, None,
                ALU.logical_shift_right), inc=False)
            drain("vector")
            dvv(lambda en: en.tensor_scalar(
                s3.bitcast(I32), s3.bitcast(I32), 8192.0, None, ALU.add),
                inc=False)
            drain("vector")
            dvv(lambda en: en.tensor_scalar(
                s3.bitcast(I32), s3.bitcast(I32), 0xFFC000, 8,
                ALU.bitwise_and, ALU.logical_shift_left), inc=False)
            drain("vector")
            dvv(lambda en: en.tensor_scalar(
                s1, s2, 0.25, -0.5, ALU.is_le, ALU.mult), inc=False)
            drain("vector")
            dvv(lambda en: en.tensor_scalar(
                s2, s2, 0.75, -0.5, ALU.is_le, ALU.mult), inc=False)
            drain("vector")
            dvv(lambda en: en.tensor_tensor(s1, s1, s2, ALU.add), inc=False)
            drain("vector")
            dvv(lambda en: en.tensor_tensor(s3, s3, s1, ALU.add), inc=False)
            dvv(lambda en: en.tensor_scalar(
                s2, src, 0.0, -0.5, ALU.is_ge, ALU.add), inc=False)
            drain("vector")
            dvv(lambda en: en.tensor_tensor(
                s2.rearrange("p (b k) -> p b k", k=16),
                s2.rearrange("p (b k) -> p b k", k=16),
                s2blk.unsqueeze(-1).broadcast_to([P, nb, 16]), ALU.mult),
                inc=False)
            drain("vector")
            v = dvv(lambda en: en.tensor_tensor(dst, s3, s2, ALU.mult))
            drain("vector")
            return v

        wq_pending = []

        def _wq_build_tile(obf_c):
            i = wq_state["i"]
            if i >= len(wq_tiles):
                return False
            wq_state["i"] = i + 1
            src, dst = wq_tiles[i]
            stg = ff(WQS_F, 1024)
            obf = bf(obf_c, 1024)
            s1, s2, s3 = ff(QSA_F, 1024), ff(QSB_F, 1024), ff(QSC_F, 1024)
            mblk, sblk = ff(SM_F, 64), ff(SM_F + 64, 64)
            rblk, s2blk = ff(SM_F + 128, 64), ff(SM_F + 192, 64)
            x3 = stg.rearrange("p (b k) -> p b k", k=16)

            def t_load():
                W("sync", "dv", wq_state["stg_free"])
                wq_state["ld"] = dma("sync", "dS", stg, src)

            def t0():
                W("vector", "dS", wq_state["ld"])
                dvv(lambda en: en.tensor_reduce(mblk, x3, AX.X, ALU.max,
                                                apply_absolute_value=True),
                    inc=False)
                drain("vector")

            def t1():
                dvv(lambda en: en.tensor_scalar(sblk, mblk, 1.0 / 6.0, 1e-12,
                                                ALU.mult, ALU.max), inc=False)
                drain("vector")

            def t2():
                dvv(lambda en: en.reciprocal(rblk, sblk), inc=False)
                dvv(lambda en: en.tensor_scalar(s2blk, sblk, 2.0, None,
                                                ALU.mult), inc=False)
                drain("vector")

            def t3():
                dvv(lambda en: en.tensor_tensor(
                    s1.rearrange("p (b k) -> p b k", k=16), x3,
                    rblk.unsqueeze(-1).broadcast_to([P, 64, 16]), ALU.mult),
                    inc=False)
                drain("vector")

            def t4():
                dvv(lambda en: en.tensor_scalar(
                    s2.bitcast(I32), s1.bitcast(I32), 0x7FFFFFFF, None,
                    ALU.bitwise_and), inc=False)
                drain("vector")

            def t5():
                dvv(lambda en: en.tensor_scalar(s3, s2, 1.0, None, ALU.max),
                    inc=False)
                drain("vector")

            def t6():
                dvv(lambda en: en.tensor_scalar(
                    s3.bitcast(I32), s3.bitcast(I32), 8, None,
                    ALU.logical_shift_right), inc=False)
                drain("vector")

            def t7():
                dvv(lambda en: en.tensor_scalar(
                    s3.bitcast(I32), s3.bitcast(I32), 8192.0, None, ALU.add),
                    inc=False)
                drain("vector")

            def t8():
                dvv(lambda en: en.tensor_scalar(
                    s3.bitcast(I32), s3.bitcast(I32), 0xFFC000, 8,
                    ALU.bitwise_and, ALU.logical_shift_left), inc=False)
                drain("vector")

            def t9():
                dvv(lambda en: en.tensor_scalar(
                    s1, s2, 0.25, -0.5, ALU.is_le, ALU.mult), inc=False)
                drain("vector")

            def t10():
                dvv(lambda en: en.tensor_scalar(
                    s2, s2, 0.75, -0.5, ALU.is_le, ALU.mult), inc=False)
                drain("vector")

            def t11():
                dvv(lambda en: en.tensor_tensor(s1, s1, s2, ALU.add),
                    inc=False)
                drain("vector")

            def t12():
                dvv(lambda en: en.tensor_tensor(s3, s3, s1, ALU.add),
                    inc=False)
                dvv(lambda en: en.tensor_scalar(
                    s2, stg, 0.0, -0.5, ALU.is_ge, ALU.add), inc=False)
                drain("vector")

            def t13():
                dvv(lambda en: en.tensor_tensor(
                    s2.rearrange("p (b k) -> p b k", k=16),
                    s2.rearrange("p (b k) -> p b k", k=16),
                    s2blk.unsqueeze(-1).broadcast_to([P, 64, 16]), ALU.mult),
                    inc=False)
                drain("vector")

            def t14():
                W("vector", "dC", wq_state["obf_free"])
                vq = dvv(lambda en: en.tensor_tensor(obf, s3, s2, ALU.mult))
                drain("vector")
                wq_state["stg_free"] = vq
                W("scalar", "dv", vq)
                wq_state["obf_free"] = dma("scalar", "dC", dst, obf)

            wq_pending.extend([t_load, t0, t1, t2, t3, t4, t5, t6, t7, t8,
                               t9, t10, t11, t12, t13, t14])
            return True

        def emit_wq_pump(nstages, obf_c):
            for _ in range(nstages):
                if not wq_pending:
                    if not _wq_build_tile(obf_c):
                        return
                wq_pending.pop(0)()

        st = {
            "ub_free_pe": 0,
            "ulora_ps_free_ac": 0,
            "sa_free_ac": [0, 0],
            "kstg_free_dc": [0, 0],
            "vstg_free_dc": [0, 0],
            "norm_ps_free_dv": 0,
            "sqb_free_pe": [0, 0],
            "xtb_last_pe": [0, 0],
            "w_last_pe": 0,
        }

        def emit_u_lora(xtbuf):
            xt3 = xtbuf.rearrange("p (a t) -> p a t", a=8)
            W("tensor", "ac", st["ulora_ps_free_ac"])
            vpe = None
            for half in range(2):
                for a in range(8):
                    def mm(en, a=a, half=half, xt3=xt3):
                        return en.matmul(
                            PS_Y0[0:32, half * 512:(half + 1) * 512],
                            laTb[:, a, :],
                            xt3[:, a, half * 512:(half + 1) * 512],
                            start=(a == 0), stop=(a == 7))
                    if half == 1 and a == 7:
                        vpe = pe(mm)
                    else:
                        pe(mm, inc=False)
            W("scalar", "pe", vpe)
            W("scalar", "pe", st["ub_free_pe"])
            acc(lambda en: en.activation(ub[0:32, 0:512],
                                         PS_Y0[0:32, 0:512], AF.Copy,
                                         scale=0.5), inc=False)
            vac = acc(lambda en: en.activation(ub[0:32, 512:1024],
                                               PS_Y0[0:32, 512:1024], AF.Copy,
                                               scale=0.5))
            st["ulora_ps_free_ac"] = vac
            return vac, vpe

        def emit_norms(vac_sq, col, sqb):
            W("tensor", "ac", vac_sq)
            W("tensor", "dv", st["norm_ps_free_dv"])
            pe(lambda en: en.matmul(PS_Y1[0:1, 0:512], onec,
                                    sqb[:, 0:512], start=True, stop=True),
               inc=False)
            vpeN = pe(lambda en: en.matmul(PS_Y1[0:1, 512:1024], onec,
                                           sqb[:, 512:1024],
                                           start=True, stop=True))
            st["sqb_free_pe"][0 if sqb is sqb_b[0] else 1] = vpeN
            W("vector", "pe", vpeN)
            dst = ff(col, 1, 0, 1)
            dvv(lambda en: en.tensor_scalar(
                ff(DUMP_F, 512, 0, 1), PS_Y1[0:1, 0:512], 1.0, None,
                ALU.mult, ALU.max, accum_out=dst), inc=False)
            drain("vector")
            vdvN = dvv(lambda en: en.tensor_scalar(
                ff(DUMP_F, 512, 0, 1), PS_Y1[0:1, 512:1024], 1.0, dst,
                ALU.mult, ALU.max, accum_out=dst))
            drain("vector")
            st["norm_ps_free_dv"] = vdvN

        ld_x = [dma("gpsimd", "dG", xtb[0].rearrange("p (a n) -> p a n", a=8), xsrc(0)), 0]
        W("tensor", "dG", v_wv)

        for pos in range(NS):
            buf = pos % 2
            if pos + 1 < NS:
                W("gpsimd", "pe", st["xtb_last_pe"][1 - buf])
                ld_x[1 - buf] = dma(
                    "gpsimd", "dG",
                    xtb[1 - buf].rearrange("p (a n) -> p a n", a=8),
                    xsrc(pos + 1))
            W("tensor", "dG", ld_x[buf])
            xt3 = xtb[buf].rearrange("p (a t) -> p a t", a=8)
            vac_u, _ = emit_u_lora(xtb[buf])
            W("tensor", "ac", vac_u)

            norms_pend = None
            for pair in range(NPAIR):
                sa = pair % 2
                ps = PS_SA if sa == 0 else PS_SB
                W("tensor", "ac", st["sa_free_ac"][sa])
                vpeK = None
                for half in range(2):
                    for a in range(8):
                        def mm(en, a=a, half=half, ps=ps, pair=pair, xt3=xt3):
                            return en.matmul(
                                ps[:, half * 512:(half + 1) * 512],
                                wkr[:, a, pair * 128:(pair + 1) * 128],
                                xt3[:, a, half * 512:(half + 1) * 512],
                                start=(a == 0), stop=False)
                        pe(mm, inc=False)
                    def mml(en, half=half, ps=ps, pair=pair):
                        return en.matmul(
                            ps[:, half * 512:(half + 1) * 512],
                            lbk[:, pair * 128:(pair + 1) * 128],
                            ub[:, half * 512:(half + 1) * 512],
                            start=False, stop=True)
                    if half == 1:
                        vpeK = pe(mml)
                    else:
                        pe(mml, inc=False)
                st["ub_free_pe"] = vpeK
                st["xtb_last_pe"][buf] = vpeK
                st["w_last_pe"] = vpeK
                kb = pair % 2
                W("scalar", "pe", vpeK)
                W("scalar", "dC", st["kstg_free_dc"][kb])
                acc(lambda en, ps=ps, kb=kb: en.activation(
                    kstg[kb][:, 0:512], ps[:, 0:512], AF.Copy), inc=False)
                acc(lambda en, ps=ps, kb=kb: en.activation(
                    kstg[kb][:, 512:1024], ps[:, 512:1024], AF.Copy),
                    inc=False)
                sq = sqb_b[pair % 2]
                W("scalar", "pe", st["sqb_free_pe"][pair % 2])
                acc(lambda en, ps=ps, sq=sq: en.activation(
                    sq[:, 0:512], ps[:, 0:512], AF.Square), inc=False)
                vacS = acc(lambda en, ps=ps, sq=sq: en.activation(
                    sq[:, 512:1024], ps[:, 512:1024], AF.Square))
                st["sa_free_ac"][sa] = vacS
                st["kstg_free_dc"][kb] = dma(
                    "scalar", "dC", ap["kt_dram"][pos, pair, :, :],
                    kstg[kb][:, :])
                if norms_pend is not None:
                    emit_norms(*norms_pend)
                norms_pend = (vacS, KMX_F + pair * 8 + pos, sq)
                if pair == NPAIR - 1:
                    emit_norms(*norms_pend)
                emit_wq_pump(3, E_C + 2 * 1024)

            for tsub in range(8):
                sa = tsub % 2
                ps = PS_SA if sa == 0 else PS_SB
                W("tensor", "ac", st["sa_free_ac"][sa])
                vpeV = None
                for half in range(2):
                    for a in range(8):
                        def mm(en, a=a, half=half, ps=ps, tsub=tsub, xt3=xt3):
                            return en.matmul(
                                ps[:, half * 512:(half + 1) * 512],
                                xt3[:, a, tsub * 128:(tsub + 1) * 128],
                                wvr[:, a, half * 512:(half + 1) * 512],
                                start=(a == 0), stop=False)
                        pe(mm, inc=False)
                    def mml(en, half=half, ps=ps, tsub=tsub):
                        return en.matmul(
                            ps[:, half * 512:(half + 1) * 512],
                            ub[:, tsub * 128:(tsub + 1) * 128],
                            lbv[:, half * 512:(half + 1) * 512],
                            start=False, stop=True)
                    if half == 1:
                        vpeV = pe(mml)
                    else:
                        pe(mml, inc=False)
                st["ub_free_pe"] = vpeV
                st["xtb_last_pe"][buf] = vpeV
                vb = tsub % 2
                W("scalar", "pe", vpeV)
                W("scalar", "dC", st["vstg_free_dc"][vb])
                if pos == 0 and tsub < 2:
                    W("scalar", "gp", v_ms)
                vs3 = vstg[vb].rearrange("p (h w) -> p h w", w=65)
                acc(lambda en, ps=ps, vs3=vs3: en.activation(
                    vs3[:, 0:8, 0:64],
                    ps[:, 0:512].rearrange("p (h d) -> p h d", d=64),
                    AF.Copy), inc=False)
                vacV = acc(lambda en, ps=ps, vs3=vs3: en.activation(
                    vs3[:, 8:16, 0:64],
                    ps[:, 512:1024].rearrange("p (h d) -> p h d", d=64),
                    AF.Copy))
                st["sa_free_ac"][sa] = vacV
                st["vstg_free_dc"][vb] = dma(
                    "scalar", "dC", ap["v_dram"][pos, tsub, :, :],
                    vstg[vb][:, :])
                emit_wq_pump(3, E_C + 2 * 1024)

        W("gpsimd", "pe", st["w_last_pe"])
        v_wq = dma("gpsimd", "dG", bf(WKR_C, 8192).rearrange("p (a n) -> p a n", a=8),
                   ap["wqT"].rearrange("(a p) o -> p a o", p=P))
        wqr = wkr
        for os_ in range(2):
            W("gpsimd", "pe", st["xtb_last_pe"][os_])
            ld = dma("gpsimd", "dG",
                     xtb[os_].rearrange("p (a n) -> p a n", a=8),
                     ap["xTown"][:, os_ * SUP:(os_ + 1) * SUP]
                     .rearrange("(a p) t -> p a t", p=P))
            if os_ == 0:
                W("tensor", "dG", v_wq)
            W("tensor", "dG", ld)
            xt3 = xtb[os_].rearrange("p (a t) -> p a t", a=8)
            vac_u, _ = emit_u_lora(xtb[os_])
            W("tensor", "ac", vac_u)
            norms_pend = None
            for pair in range(NPAIR):
                sa = pair % 2
                ps = PS_SA if sa == 0 else PS_SB
                W("tensor", "ac", st["sa_free_ac"][sa])
                vpeQ = None
                for half in range(2):
                    for a in range(8):
                        def mm(en, a=a, half=half, ps=ps, pair=pair, xt3=xt3):
                            return en.matmul(
                                ps[:, half * 512:(half + 1) * 512],
                                wqr[:, a, pair * 128:(pair + 1) * 128],
                                xt3[:, a, half * 512:(half + 1) * 512],
                                start=(a == 0), stop=False)
                        pe(mm, inc=False)
                    def mml(en, half=half, ps=ps, pair=pair):
                        return en.matmul(
                            ps[:, half * 512:(half + 1) * 512],
                            lbq[:, pair * 128:(pair + 1) * 128],
                            ub[:, half * 512:(half + 1) * 512],
                            start=False, stop=True)
                    if half == 1:
                        vpeQ = pe(mml)
                    else:
                        pe(mml, inc=False)
                st["ub_free_pe"] = vpeQ
                st["xtb_last_pe"][os_] = vpeQ
                W("scalar", "pe", vpeQ)
                qtdst = bf(QT_C + pair * 2048 + os_ * 1024, 1024)
                acc(lambda en, ps=ps, qtdst=qtdst: en.activation(
                    qtdst[:, 0:512], ps[:, 0:512], AF.Copy), inc=False)
                acc(lambda en, ps=ps, qtdst=qtdst: en.activation(
                    qtdst[:, 512:1024], ps[:, 512:1024], AF.Copy), inc=False)
                sq = sqb_b[pair % 2]
                W("scalar", "pe", st["sqb_free_pe"][pair % 2])
                acc(lambda en, ps=ps, sq=sq: en.activation(
                    sq[:, 0:512], ps[:, 0:512], AF.Square), inc=False)
                vacS = acc(lambda en, ps=ps, sq=sq: en.activation(
                    sq[:, 512:1024], ps[:, 512:1024], AF.Square))
                st["sa_free_ac"][sa] = vacS
                if norms_pend is not None:
                    emit_norms(*norms_pend)
                norms_pend = (vacS, QMX_F + pair * 2 + os_, sq)
                if pair == NPAIR - 1:
                    emit_norms(*norms_pend)

        while wq_pending:
            emit_wq_pump(16, E_C + 2 * 1024)
        drain("vector")
        for pair in range(NPAIR):
            dvv(lambda en, pair=pair: en.tensor_reduce(
                ff(KM2_F + pair, 1, 0, 1), ff(KMX_F + pair * 8, 8, 0, 1),
                AX.X, ALU.max), inc=False)
            dvv(lambda en, pair=pair: en.tensor_reduce(
                ff(QM2_F + pair, 1, 0, 1), ff(QMX_F + pair * 2, 2, 0, 1),
                AX.X, ALU.max), inc=False)
        drain("vector")
        vdv = dvv(lambda en: en.tensor_tensor(
            ff(CSC_F, 8, 0, 1), ff(KM2_F, 8, 0, 1), ff(QM2_F, 8, 0, 1),
            ALU.mult))
        W("scalar", "dv", vdv)
        acc(lambda en: en.activation(ff(CSC_F, 8, 0, 1),
                                     ff(CSC_F, 8, 0, 1), AF.Ln), inc=False)
        drain("scalar")
        vac = acc(lambda en: en.activation(
            ff(CSC_F, 8, 0, 1), ff(CSC_F, 8, 0, 1), AF.Exp, scale=0.5))
        W("vector", "ac", vac)
        drain("vector")
        vdv = dvv(lambda en: en.tensor_scalar(
            ff(CSC_F, 8, 0, 1), ff(CSC_F, 8, 0, 1), -0.126256, None,
            ALU.mult))
        drain("vector")
        W("scalar", "dv", vdv)
        vst = dma("scalar", "dC", ap["cneg_dram"][:, :], ff(CSC_F, 8, 0, 1))
        if debug:
            dma("scalar", "dC", ap["dbg_cneg"][:, :], ff(CSC_F, 8, 0, 1))
        W("sync", "dC", vst)
        vld = dma("sync", "dS", ff(CNEG_F, 8),
                  ap["cneg_dram"][0:1, :].broadcast_to([P, 8]))
        W("vector", "dS", vld)
        W("vector", "dS", v_sent)
        for slot in range(2):
            for pos in range(COVER[slot]):
                for pair in range(NPAIR):
                    idx = BIAS_F + slot * 64 + pos * 8 + pair
                    dvv(lambda en, idx=idx, pair=pair, slot=slot, pos=pos:
                        en.tensor_scalar(
                            ff(idx, 1), ff(CNEG_F + pair, 1),
                            ff(SENT_F + slot * 8 + pos, 1), None, ALU.min),
                        inc=False)
        drain("vector")
        vdv_bias = dvv(lambda en: en.tensor_scalar(
            ff(SM_F + 500, 1), ff(BIAS_F, 1), 1.0, None, ALU.mult))

        W("sync", "pe", st["ub_free_pe"])
        v_msk = dma("sync", "dS", bf(MSK_C, 8192), ap["dmask"][:, :])
        W("sync", "dC", sems["dC"].v)

        ktb = [bf(KT0_C, 8192), bf(KT1_C, 8192)]
        vbuf = [bf(V0_C, 8320), bf(V1_C, 8320)]
        Ebuf = [[bf(E_C + i * 1024, 1024) for i in range(3)],
                [bf(E_C + (3 + i) * 1024, 1024) for i in range(3)]]

        def qtv(pair, slot, r0, r1):
            return bf(QT_C + pair * 2048 + slot * 1024, 1024, r0, r1 - r0)

        def prime(n=24):
            for i in range(n):
                def mm(en, i=i):
                    return en.matmul(PS_SA[:, 0:512], bf(QT_C, 128),
                                     bf(QT_C, 512), start=True, stop=True)
                pe(mm, inc=False)
            return pe(lambda en: en.matmul(PS_SA[:, 0:512], bf(QT_C, 128),
                                           bf(QT_C, 512), start=True,
                                           stop=True))

        st2 = {
            "eb": 0,
            "e_free_pe": [[0, 0, 0], [0, 0, 0]],
            "s_free": [st["sa_free_ac"][0], st["sa_free_ac"][1]],
            "psY_evac_dv": st["norm_ps_free_dv"],
            "psY0_ac": st["ulora_ps_free_ac"],
            "x1q_free": 0,
            "kt_last_pe": [0, 0],
            "proj_last_pe": 0,
        }
        kv_ld = {}

        def load_kv(pos, buf):
            W("sync", "pe", st2["kt_last_pe"][buf])
            dma("sync", "dS", ktb[buf].rearrange("p (r n) -> p r n", r=8),
                ap["kt_dram"][pos].rearrange("r p n -> p r n"))
            return dma("sync", "dS",
                       vbuf[buf].rearrange("p (s n) -> p s n", s=8),
                       ap["v_dram"][pos].rearrange("s p n -> p s n"))

        kv_ld[(0, 0)] = load_kv(0, 0)
        W("tensor", "dS", v_msk)
        W("scalar", "dC", wq_state["obf_free"])
        W("scalar", "dv", vdv_bias)
        W("tensor", "ac", st2["psY0_ac"])

        def attention_slot(slot):
            cov = COVER[slot]
            W("tensor", "ac", st2["psY0_ac"])
            W("tensor", "ac", st2["s_free"][0])
            vpr = prime()
            W("scalar", "pe", vpr)
            for pos in range(cov):
                buf = pos % 2
                if (slot, pos) not in kv_ld:
                    kv_ld[(slot, pos)] = load_kv(pos, buf)
                if pos + 1 < cov:
                    kv_ld[(slot, pos + 1)] = load_kv(pos + 1, 1 - buf)
                W("tensor", "dS", kv_ld[(slot, pos)])
                kt3 = ktb[buf].rearrange("p (r n) -> p r n", r=8)
                v3 = vbuf[buf].rearrange("p (s n) -> p s n", s=8)
                diag = pos == DIAGPOS[slot]
                vpeAV = None

                def emit_av(ksub, eb, gate, pair, v3, first):
                    if gate[0] is not None:
                        W("tensor", gate[0], gate[1])
                    if first:
                        W("tensor", "dv", st2["psY_evac_dv"])
                    vv = None
                    for qsub in range(8):
                        pst = PSY[qsub // 4]
                        coff = (0, 130, 512, 642)[qsub % 4]
                        fb = (first and qsub % 2 == 0)
                        for h in range(2):
                            def av(en, h=h, qsub=qsub, ksub=ksub, eb=eb,
                                   pst=pst, coff=coff, v3=v3, pair=pair,
                                   fb=fb):
                                return en.matmul(
                                    pst[:, coff + h * 65:coff + h * 65 + 65],
                                    Ebuf[h][eb]
                                    [:, qsub * 128:(qsub + 1) * 128],
                                    v3[:, ksub,
                                       (pair * 2 + h) * 65:
                                       (pair * 2 + h) * 65 + 65],
                                    start=(fb and h == 0), stop=(ksub == 7))
                            if h == 1 and qsub == 7:
                                vv = pe(av)
                            else:
                                pe(av, inc=False)
                    st2["e_free_pe"][0][eb] = vv
                    st2["e_free_pe"][1][eb] = vv
                    return vv

                for pair in range(NPAIR):
                    bias = ff(BIAS_F + slot * 64 + pos * 8 + pair, 1)
                    pend = None
                    for ksub in range(8):
                        eb = st2["eb"]
                        st2["eb"] = (eb + 1) % 3
                        vpeSA = vpeSB = None
                        W("tensor", "ac", st2["s_free"][0])
                        for half in range(2):
                            def mmSA(en, half=half, pair=pair, ksub=ksub,
                                     kt3=kt3, slot=slot):
                                return en.matmul(
                                    PS_SA[:, half * 512:(half + 1) * 512],
                                    kt3[0:64, pair,
                                        ksub * 128:(ksub + 1) * 128],
                                    qtv(pair, slot, 0, 64)
                                    [:, half * 512:(half + 1) * 512],
                                    start=True, stop=True,
                                    tile_position=(0, 0))
                            if half == 1:
                                vpeSA = pe(mmSA)
                            else:
                                pe(mmSA, inc=False)
                        W("tensor", "ac", st2["s_free"][1])
                        for half in range(2):
                            def mmSB(en, half=half, pair=pair, ksub=ksub,
                                     kt3=kt3, slot=slot):
                                return en.matmul(
                                    PS_SB[:, half * 512:(half + 1) * 512],
                                    kt3[64:128, pair,
                                        ksub * 128:(ksub + 1) * 128],
                                    qtv(pair, slot, 64, 128)
                                    [:, half * 512:(half + 1) * 512],
                                    start=True, stop=True,
                                    tile_position=(64, 0))
                            if half == 1:
                                vpeSB = pe(mmSB)
                            else:
                                pe(mmSB, inc=False)
                        st2["kt_last_pe"][buf] = vpeSB
                        if pend is not None:
                            vpeAV = emit_av(pend[0], pend[1], pend[2], pair,
                                            v3, pend[0] == 0)
                        W("scalar", "pe", vpeSA)
                        W("scalar", "pe", st2["e_free_pe"][0][eb])
                        vacA = acc(lambda en, eb=eb, bias=bias: en.activation(
                            Ebuf[0][eb][:, :], PS_SA[:, :], AF.Exp,
                            bias=bias, scale=0.125))
                        st2["s_free"][0] = vacA
                        W("scalar", "pe", vpeSB)
                        W("scalar", "pe", st2["e_free_pe"][1][eb])
                        vacB = acc(lambda en, eb=eb, bias=bias: en.activation(
                            Ebuf[1][eb][:, :], PS_SB[:, :], AF.Exp,
                            bias=bias, scale=0.125))
                        st2["s_free"][1] = vacB
                        gate = ("ac", vacB)
                        if diag:
                            W("vector", "ac", vacB)
                            mview = bf(MSK_C + ksub * 1024, 1024)
                            dvv(lambda en, eb=eb, mview=mview:
                                en.tensor_tensor(
                                    Ebuf[0][eb][:, :], Ebuf[0][eb][:, :],
                                    mview, ALU.mult), inc=False)
                            vmask = dvv(lambda en, eb=eb, mview=mview:
                                        en.tensor_tensor(
                                            Ebuf[1][eb][:, :],
                                            Ebuf[1][eb][:, :],
                                            mview, ALU.mult))
                            drain("vector")
                            gate = ("dv", vmask)
                        pend = (ksub, eb, gate)
                    vpeAV = emit_av(pend[0], pend[1], pend[2], pair, v3,
                                    False)
                    W("vector", "pe", vpeAV)
                    vdvY = None
                    for t in range(2):
                        src = PSY[t][:, :].rearrange(
                            "p (a r) -> p a r", a=2)[:, :, 0:260].rearrange(
                            "p a (b c) -> p a b c", c=130)
                        base = YACC_F + ((t * 4) * 8 + pair) * 130
                        dstsl = ARF[:, base:base + 4 * 1040].rearrange(
                            "p (a b r) -> p a b r", a=2, r=1040)[:, :, :, 0:130]
                        if pos == 0:
                            def ev(en, dstsl=dstsl, src=src):
                                return en.tensor_copy(dstsl, src)
                        else:
                            def ev(en, dstsl=dstsl, src=src):
                                return en.tensor_tensor(dstsl, dstsl, src,
                                                        ALU.add)
                        vdvY = dvv(ev)
                    drain("vector")
                    st2["psY_evac_dv"] = vdvY
                emit_wq_pump(8, XQR_C)

        wp_ld = {}

        def slot_post(slot):
            while wq_pending:
                emit_wq_pump(16, XQR_C)
            yn_st = 0
            for qsub in range(8):
                base = YACC_F + qsub * 8 * 130
                yv = ARF[:, base:base + 8 * 130]
                dens = yv.rearrange("p (r c) -> p r c", c=130)[:, :, 64:130:65]
                rec = ff(RECN_F, 16).rearrange("p (r c) -> p r c", c=2)
                W("vector", "dv", st2["psY_evac_dv"])
                if qsub == 0:
                    drain("vector")
                dvv(lambda en, dens=dens, rec=rec: en.reciprocal(rec, dens),
                    inc=False)
                drain("vector")
                ys = yv.rearrange("p (r hc) -> p r hc", hc=130).rearrange(
                    "p r (h c) -> p r h c", c=65)[:, :, :, 0:64]
                yn = bf(YN_STG_C, 1024).rearrange(
                    "p (r h c) -> p r h c", r=8, c=64)
                if qsub > 0:
                    W("vector", "dC", yn_st)
                vdvN = dvv(lambda en, ys=ys, yn=yn, rec=rec: en.tensor_tensor(
                    yn, ys,
                    rec.unsqueeze(-1).broadcast_to([P, 8, 2, 64]), ALU.mult))
                drain("vector")
                W("scalar", "dv", vdvN)
                yn_st = dma("scalar", "dC",
                            ap["ynorm_dram"][slot, qsub * P:(qsub + 1) * P, :],
                            bf(YN_STG_C, 1024))
                if debug:
                    dma("scalar", "dC",
                        ap["dbg_yn"][slot, qsub * P:(qsub + 1) * P, :],
                        bf(YN_STG_C, 1024))
            W("sync", "dC", yn_st)
            W("sync", "pe", st2["kt_last_pe"][0])
            ynld = 0
            for cc in range(8):
                ynld = dma("sync", "dS", bf(YNT_C + cc * 1024, 1024),
                           ap["ynorm_dram"][slot][:, cc * P:(cc + 1) * P],
                           transpose=True)
            ynt3 = bf(YNT_C, 8192).rearrange("p (a t) -> p a t", a=8)
            W("tensor", "dS", ynld)
            W("tensor", "dv", st2["psY_evac_dv"])
            vpeU = None
            for half in range(2):
                for a in range(8):
                    def mmu(en, a=a, half=half, ynt3=ynt3):
                        return en.matmul(
                            PS_Y0[0:32, half * 512:(half + 1) * 512],
                            bf(LAPT_C + a * 32, 32),
                            ynt3[:, a, half * 512:(half + 1) * 512],
                            start=(a == 0), stop=(a == 7))
                    if half == 1 and a == 7:
                        vpeU = pe(mmu)
                    else:
                        pe(mmu, inc=False)
            W("scalar", "pe", vpeU)
            W("scalar", "pe", st2["proj_last_pe"])
            acc(lambda en: en.activation(u2b[0:32, 0:512],
                                         PS_Y0[0:32, 0:512], AF.Copy,
                                         scale=0.5), inc=False)
            vacU2 = acc(lambda en: en.activation(u2b[0:32, 512:1024],
                                                 PS_Y0[0:32, 512:1024],
                                                 AF.Copy, scale=0.5))
            st2["psY0_ac"] = vacU2
            W("gpsimd", "pe", st2["kt_last_pe"][1])
            for a in range(8):
                wp_ld[(slot, a)] = dma(
                    "gpsimd", "dG", bf(KT1_C + a * 1024, 1024),
                    ap["wpT"][a * P:(a + 1) * P, :])
            W("tensor", "ac", vacU2)
            W("tensor", "ac", st2["s_free"][0])
            prime()
            for tsub in range(8):
                sa = tsub % 2
                ps = PS_SA if sa == 0 else PS_SB
                W("tensor", "ac", st2["s_free"][sa])
                W("tensor", "dv", st2.get("s_free_dv", [0, 0])[sa])
                vpeP = None
                for a in range(8):
                    if tsub == 0:
                        W("tensor", "dG", wp_ld[(slot, a)])
                    for half in range(2):
                        def mmp(en, a=a, half=half, ps=ps, tsub=tsub):
                            return en.matmul(
                                ps[:, half * 512:(half + 1) * 512],
                                ynt3[:, a, tsub * 128:(tsub + 1) * 128],
                                bf(KT1_C + a * 1024, 1024)
                                [:, half * 512:(half + 1) * 512],
                                start=(a == 0), stop=False)
                        pe(mmp, inc=False)
                for half in range(2):
                    def mmpl(en, half=half, ps=ps, tsub=tsub):
                        return en.matmul(
                            ps[:, half * 512:(half + 1) * 512],
                            u2b[:, tsub * 128:(tsub + 1) * 128],
                            lbp[:, half * 512:(half + 1) * 512],
                            start=False, stop=True)
                    if half == 1:
                        vpeP = pe(mmpl)
                    else:
                        pe(mmpl, inc=False)
                st2["proj_last_pe"] = vpeP
                st2["kt_last_pe"][0] = vpeP
                st2["kt_last_pe"][1] = vpeP
                W("scalar", "pe", vpeP)
                W("scalar", "dv", st2.get("pstg_free_dv", 0))
                W("scalar", "dC", st2.get("pstg_free_dc", 0))
                acc(lambda en, ps=ps: en.activation(
                    ff(PSTG_F, 512), ps[:, 0:512], AF.Copy), inc=False)
                vacP = acc(lambda en, ps=ps: en.activation(
                    ff(PSTG_F + 512, 512), ps[:, 512:1024], AF.Copy))
                st2["s_free"][sa] = vacP
                row0 = slot * 1024 + tsub * P
                W("sync", "dv", st2.get("pstg2_free_dv", 0))
                vxr = dma("sync", "dS", ff(PSTG2_F, 1024),
                          ap["xres"][row0:row0 + P, :])
                W("vector", "ac", vacP)
                if debug:
                    vdbg = dma("scalar", "dC", ap["dbg_p"][row0:row0 + P, :],
                               ff(PSTG_F, 1024))
                    W("vector", "dC", vdbg)
                W("vector", "dS", vxr)
                vres = dvv(lambda en: en.tensor_tensor(
                    ff(PSTG_F, 1024), ff(PSTG_F, 1024), ff(PSTG2_F, 1024),
                    ALU.add))
                st2["pstg2_free_dv"] = vres
                drain("vector")
                dvv(lambda en: en.bn_stats(ff(BN_F, 6), ff(PSTG_F, 512)),
                    inc=False)
                dvv(lambda en: en.bn_stats(ff(BN_F + 6, 6),
                                           ff(PSTG_F + 512, 512)), inc=False)
                drain("vector")
                vdvS = dvv(lambda en: en.bn_aggr(ff(BN_F + 12, 2),
                                                 ff(BN_F, 12)))
                drain("vector")
                dvv(lambda en: en.tensor_scalar(
                    ff(BN_F + 13, 1), ff(BN_F + 13, 1), LNEPS, None,
                    ALU.add), inc=False)
                drain("vector")
                vdvS = dvv(lambda en: en.tensor_scalar(
                    ff(BN_F + 15, 1), ff(BN_F + 13, 1), 1.0, None, ALU.mult))
                W("scalar", "dv", vdvS)
                acc(lambda en: en.activation(
                    ff(BN_F + 14, 1), ff(BN_F + 13, 1), AF.Ln), inc=False)
                drain("scalar")
                vacR = acc(lambda en: en.activation(
                    ff(BN_F + 14, 1), ff(BN_F + 14, 1), AF.Exp, scale=-0.5))
                W("vector", "ac", vacR)
                vdvX = dvv(lambda en: en.tensor_scalar(
                    ff(PSTG_F, 1024), ff(PSTG_F, 1024), ff(BN_F + 12, 1),
                    ff(BN_F + 14, 1), ALU.subtract, ALU.mult))
                drain("vector")
                W("scalar", "dv", vdvX)
                vx1 = dma("scalar", "dC", ap["x1_dram"][row0:row0 + P, :],
                          ff(PSTG_F, 1024))
                if debug:
                    dma("scalar", "dC", ap["dbg_x1"][row0:row0 + P, :],
                        ff(PSTG_F, 1024))
                W("vector", "dC", wq_state["obf_free"])
                W("vector", "dC", st2["x1q_free"])
                vq = emit_quant(ff(PSTG_F, 1024), bf(XQR_C, 1024), 1024,
                                ff(QSA_F, 1024), ff(QSB_F, 1024),
                                ff(QSC_F, 1024),
                                ff(SM_F, 64), ff(SM_F + 64, 64),
                                ff(SM_F + 128, 64), ff(SM_F + 192, 64))
                wq_state["stg_free"] = vq
                st2["pstg_free_dv"] = vq
                st2["pstg_free_dc"] = vx1
                W("scalar", "dv", vq)
                vxq = dma("scalar", "dC", ap["xq_dram"][row0:row0 + P, :],
                          bf(XQR_C, 1024))
                if debug:
                    dma("scalar", "dC", ap["dbg_xq"][row0:row0 + P, :],
                        bf(XQR_C, 1024))
                wq_state["obf_free"] = vxq
                st2["x1q_free"] = vxq

        def post_bridge():
            W("scalar", "dv", st2.get("s_free_dv", [0, 0])[0])
            W("scalar", "dv", st2.get("s_free_dv", [0, 0])[1])
            v = acc(lambda en: en.activation(ff(BN_F + 15, 1),
                                             ff(BN_F + 15, 1), AF.Copy))
            st2["s_free"][0] = v
            st2["s_free"][1] = v

        attention_slot(0)
        slot_post(0)
        post_bridge()
        attention_slot(1)
        slot_post(1)
        post_bridge()
        while wq_state["i"] < len(wq_tiles) or wq_pending:
            emit_wq_pump(16, XQR_C)

        W("sync", "dC", wq_state["obf_free"])
        tstg = bf(WBG0_C, I_FFN)
        vtr = 0
        for src_d, dst_d in (("wgq_dram", "wgqT_dram"),
                             ("wuq_dram", "wuqT_dram")):
            for ccx in range(8):
                vl = dma("sync", "dS", tstg,
                         ap[src_d][:, ccx * P:(ccx + 1) * P], transpose=True)
                W("scalar", "dS", vl)
                vtr = dma("scalar", "dC", ap[dst_d][ccx], tstg)
                W("sync", "dC", vtr)
        for r in range(C // P):
            stg = ff(H0_F, I_FFN)
            obf = bf(WDQ3_C, I_FFN)
            W("sync", "dv", wq_state["stg_free"])
            ld = dma("sync", "dS", stg, ap["wd"][r * P:(r + 1) * P, :])
            W("vector", "dS", ld)
            W("vector", "dC", wq_state["obf_free"])
            vq = emit_quant(stg, obf, I_FFN,
                            ff(Q31_F, I_FFN), ff(Q32_F, I_FFN),
                            ff(Q33_F, I_FFN),
                            ff(SM_F, 176), ff(SM_F + 176, 176),
                            ff(R3A_F, 176), ff(R3B_F, 176))
            wq_state["stg_free"] = vq
            W("scalar", "dv", vq)
            wq_state["obf_free"] = dma("scalar", "dC",
                                       ap["wdq_dram"][r * P:(r + 1) * P, :],
                                       obf)

        W("sync", "dC", sems["dC"].v)
        for ccx in range(NIC):
            vl = dma("sync", "dS", tstg[:, 0:C],
                     ap["wdq_dram"][:, ccx * P:(ccx + 1) * P], transpose=True)
            W("scalar", "dS", vl)
            vtr = dma("scalar", "dC", ap["wdqT_dram"][ccx], tstg[:, 0:C])
            W("sync", "dC", vtr)
        wdqT_done = vtr
        W("sync", "dC", sems["dC"].v)
        v_xqt = 0
        for cc in range(8):
            v_xqt = dma("sync", "dS", bf(XQT_C + cc * 2048, 2048),
                        ap["xq_dram"][:, cc * P:(cc + 1) * P], transpose=True)
        xqt3 = bf(XQT_C, 16384).rearrange("p (a t) -> p a t", a=8)

        IBANDS = [(0, 512), (512, 512), (1024, 512), (1536, 512),
                  (2048, 512), (2560, 256)]
        wb_g = [bf(WBG0_C, 4096), bf(WBG1_C, 4096)]
        wb_u = [bf(WBU0_C, 4096), bf(WBU1_C, 4096)]
        st3 = {"wb_free_pe": [0, 0], "hq_free_dc": 0,
               "gu_free_ac": [0, 0], "gu_free_dv": [0, 0],
               "h_free": [0, 0]}

        def load_band2(ib, wb):
            i0, wdt = IBANDS[ib]
            W("sync", "pe", st3["wb_free_pe"][wb])
            dma("sync", "dS",
                wb_g[wb].rearrange("p (a w) -> p a w", a=8)[:, :, 0:wdt],
                ap["wgqT_dram"][:, :, i0:i0 + wdt].rearrange(
                    "a p w -> p a w"))
            return dma("sync", "dS",
                       wb_u[wb].rearrange("p (a w) -> p a w", a=8)
                       [:, :, 0:wdt],
                       ap["wuqT_dram"][:, :, i0:i0 + wdt].rearrange(
                           "a p w -> p a w"))

        hbufs = (ff(H0_F, I_FFN), ff(H1_F, I_FFN))
        PSG = (PS_SA, PS_SB)
        PSU = (PS_Y0, PS_Y1)
        W("sync", "dC", wdqT_done)
        band_ld = {(0, 0): load_band2(0, 0)}
        W("tensor", "ac", st2["s_free"][0])
        prime()

        for grp in range(8):
            for ib in range(6):
                wb = ib % 2
                i0, wdt = IBANDS[ib]
                ldv = band_ld.pop((grp, ib))
                if ib + 1 < 6:
                    band_ld[(grp, ib + 1)] = load_band2(ib + 1, 1 - wb)
                elif grp + 1 < 8:
                    band_ld[(grp + 1, 0)] = load_band2(0, 1 - wb)
                W("tensor", "dS", ldv)
                if grp == 0 and ib == 0:
                    W("tensor", "dS", v_xqt)
                for ts in range(2):
                    tsub = grp * 2 + ts
                    psg, psu = PSG[ts], PSU[ts]
                    W("tensor", "ac", st3["gu_free_ac"][ts])
                    W("tensor", "dv", st3["gu_free_dv"][ts])
                    vpeGU = None
                    for a in range(8):
                        def mmg(en, a=a, tsub=tsub, wb=wb, wdt=wdt, psg=psg):
                            return en.matmul(
                                psg[:, 0:wdt],
                                xqt3[:, a, tsub * 128:(tsub + 1) * 128],
                                wb_g[wb][:, a * 512:a * 512 + wdt],
                                start=(a == 0), stop=(a == 7))
                        pe(mmg, inc=False)
                        def mmu(en, a=a, tsub=tsub, wb=wb, wdt=wdt, psu=psu):
                            return en.matmul(
                                psu[:, 0:wdt],
                                xqt3[:, a, tsub * 128:(tsub + 1) * 128],
                                wb_u[wb][:, a * 512:a * 512 + wdt],
                                start=(a == 0), stop=(a == 7))
                        if a == 7:
                            vpeGU = pe(mmu)
                        else:
                            pe(mmu, inc=False)
                    st3["wb_free_pe"][wb] = vpeGU
                    W("scalar", "pe", vpeGU)
                    W("scalar", "dv", st3.get("sil_free_dv", 0))
                    vsil = acc(lambda en, psg=psg, wdt=wdt: en.activation(
                        ff(SIL_F, wdt), psg[:, 0:wdt], AF.Silu))
                    st3["gu_free_ac"][ts] = vsil
                    W("vector", "ac", vsil)
                    W("vector", "dv", st3["h_free"][ts])
                    vh = dvv(lambda en, ts=ts, i0=i0, wdt=wdt, psu=psu:
                             en.tensor_tensor(
                                 hbufs[ts][:, i0:i0 + wdt],
                                 ff(SIL_F, wdt), psu[:, 0:wdt], ALU.mult))
                    st3["gu_free_dv"][ts] = vh
                    st3["sil_free_dv"] = vh
                    drain("vector")
            for ts in range(2):
                tsub = grp * 2 + ts
                W("vector", "dC", st3["hq_free_dc"])
                vq = emit_quant(hbufs[ts], bf(HQS_C, I_FFN), I_FFN,
                                ff(Q31_F, I_FFN), ff(Q32_F, I_FFN),
                                ff(Q33_F, I_FFN),
                                ff(SM_F, 176), ff(SM_F + 176, 176),
                                ff(R3A_F, 176), ff(R3B_F, 176))
                st3["h_free"][ts] = vq
                W("scalar", "dv", vq)
                st3["hq_free_dc"] = dma(
                    "scalar", "dC",
                    ap["hq_dram"][tsub * P:(tsub + 1) * P, :],
                    bf(HQS_C, I_FFN))

        W("sync", "dC", sems["dC"].v)
        W("sync", "dv", sems["dv"].v)
        W("sync", "pe", st3["wb_free_pe"][0])
        W("sync", "pe", st3["wb_free_pe"][1])
        vtr = 0
        for icx in range(NIC):
            vl = dma("sync", "dS", bf(WBG0_C, 2048),
                     ap["hq_dram"][:, icx * P:(icx + 1) * P], transpose=True)
            W("scalar", "dS", vl)
            vtr = dma("scalar", "dC", ap["hqT_dram"][icx], bf(WBG0_C, 2048))
            W("sync", "dC", vtr)
        hqt = [bf(HQT0_C, 2048), bf(HQT1_C, 2048)]
        wdt_ = [bf(WDT0_C, 1024), bf(WDT1_C, 1024)]
        st4 = {"ic_free_pe": [0, 0], "ps_free": [0, 0, 0, 0]}
        PSD = (PS_SA, PS_SB, PS_Y0, PS_Y1)

        def load_ic(ic, buf):
            W("sync", "pe", st4["ic_free_pe"][buf])
            dma("sync", "dS", hqt[buf], ap["hqT_dram"][ic])
            return dma("sync", "dS", wdt_[buf], ap["wdqT_dram"][ic])

        ld_next = None
        W("tensor", "ac", st3["gu_free_ac"][0])
        prime()
        for tg in range(4):
            ld_ic = {0: load_ic(0, 0)} if ld_next is None else ld_next
            ld_next = None
            vpeD = None
            for ic in range(NIC):
                buf = ic % 2
                if ic + 1 < NIC:
                    ld_ic[ic + 1] = load_ic(ic + 1, 1 - buf)
                elif tg + 1 < 4:
                    ld_next = {0: load_ic(0, 1 - buf)}
                W("tensor", "dS", ld_ic[ic])
                for ts in range(4):
                    tsub = tg * 4 + ts
                    ps = PSD[ts]
                    if ic == 0:
                        W("tensor", "ac", st4["ps_free"][ts])
                    for half in range(2):
                        def mmd(en, half=half, ic=ic, tsub=tsub, buf=buf,
                                ps=ps):
                            return en.matmul(
                                ps[:, half * 512:(half + 1) * 512],
                                hqt[buf][:, tsub * 128:tsub * 128 + 128],
                                wdt_[buf][:, half * 512:(half + 1) * 512],
                                start=(ic == 0), stop=(ic == NIC - 1))
                        if half == 1 and ts == 3:
                            vpeD = pe(mmd)
                        else:
                            pe(mmd, inc=False)
                st4["ic_free_pe"][buf] = vpeD
            for ts in range(4):
                tsub = tg * 4 + ts
                ps = PSD[ts]
                row0 = tsub * P
                W("scalar", "pe", vpeD)
                W("scalar", "dC", st4.get("pstg3_free_dc", 0))
                W("scalar", "dv", st4.get("pstg3_free_dv", 0))
                acc(lambda en, ps=ps: en.activation(
                    ff(PSTG3_F, 512), ps[:, 0:512], AF.Copy), inc=False)
                vacD = acc(lambda en, ps=ps: en.activation(
                    ff(PSTG3_F + 512, 512), ps[:, 512:1024], AF.Copy))
                st4.setdefault("ps_free_dv", [0, 0, 0, 0])
                st4["ps_free"][ts] = vacD
                W("sync", "dv", st4.get("x1s_free_dv", 0))
                vx1 = dma("sync", "dS", ff(X1S_F, 1024),
                          ap["x1_dram"][row0:row0 + P, :])
                W("vector", "ac", vacD)
                W("vector", "dS", vx1)
                vres2 = dvv(lambda en: en.tensor_tensor(
                    ff(PSTG3_F, 1024), ff(PSTG3_F, 1024), ff(X1S_F, 1024),
                    ALU.add))
                st4["x1s_free_dv"] = vres2
                drain("vector")
                dvv(lambda en: en.bn_stats(ff(BN3_F, 6), ff(PSTG3_F, 512)),
                    inc=False)
                dvv(lambda en: en.bn_stats(ff(BN3_F + 6, 6),
                                           ff(PSTG3_F + 512, 512)),
                    inc=False)
                drain("vector")
                dvv(lambda en: en.bn_aggr(ff(BN3_F + 12, 2), ff(BN3_F, 12)),
                    inc=False)
                drain("vector")
                dvv(lambda en: en.tensor_scalar(
                    ff(BN3_F + 13, 1), ff(BN3_F + 13, 1), LNEPS, None,
                    ALU.add), inc=False)
                drain("vector")
                vdvS = dvv(lambda en: en.tensor_scalar(
                    ff(BN3_F + 15, 1), ff(BN3_F + 13, 1), 1.0, None,
                    ALU.mult))
                W("scalar", "dv", vdvS)
                acc(lambda en: en.activation(
                    ff(BN3_F + 14, 1), ff(BN3_F + 13, 1), AF.Ln), inc=False)
                drain("scalar")
                vacR = acc(lambda en: en.activation(
                    ff(BN3_F + 14, 1), ff(BN3_F + 14, 1), AF.Exp,
                    scale=-0.5))
                W("vector", "ac", vacR)
                vdvX = dvv(lambda en: en.tensor_scalar(
                    ff(PSTG3_F, 1024), ff(PSTG3_F, 1024), ff(BN3_F + 12, 1),
                    ff(BN3_F + 14, 1), ALU.subtract, ALU.mult))
                st4["pstg3_free_dv"] = vdvX
                drain("vector")
                W("scalar", "dv", vdvX)
                st4["pstg3_free_dc"] = dma(
                    "scalar", "dC", ap["out"][row0:row0 + P, :],
                    ff(PSTG3_F, 1024))

        W("sync", "dC", sems["dC"].v)
        W("gpsimd", "dC", sems["dC"].v)

        @block.sync
        def _(en):
            for f in ops["sync"]:
                f(en)

        @block.gpsimd
        def _(en):
            for f in ops["gpsimd"]:
                f(en)

        @block.tensor
        def _(en):
            for f in ops["tensor"]:
                f(en)

        @block.vector
        def _(en):
            for f in ops["vector"]:
                f(en)

        @block.scalar
        def _(en):
            for f in ops["scalar"]:
                f(en)

    return nc, ap


def kernel(**inputs):
    in_maps = _host_prepare(inputs)
    nc, _ = build_program(debug=False)
    res = run_bass_kernel_spmd(nc, in_maps, core_ids=list(range(8)))
    outp = np.empty((2, T, C), np.float32)
    for core in range(8):
        b, g = core // 4, core % 4
        o = np.asarray(res.results[core]["out"])
        outp[b, g * SUP:(g + 1) * SUP] = o[0:SUP]
        outp[b, (7 - g) * SUP:(8 - g) * SUP] = o[SUP:2 * SUP]
    return outp
